# version 07
# baseline (speedup 1.0000x reference)
import math
import os
from contextlib import ExitStack

import numpy as np
import ml_dtypes

import concourse.bass as bass
import concourse.mybir as mybir
from concourse.bass_utils import run_bass_kernel_spmd

F32 = mybir.dt.float32
BF16 = mybir.dt.bfloat16
AF = mybir.ActivationFunctionType
ALU = mybir.AluOpType
AX = mybir.AxisListType.X

P = 128
S = 2048
D = 1024
H = 16
DH = 64
E8 = 8
FF = 2048
NTB = S // P
NDC = D // P
NKC = S // P
C_CLIP = float(np.float32(1.0 - 1e-6))
LN_BIAS = float(1.0 + math.pi / 2.0)
L2EPS2 = float(2.0 * math.log(1e-12))
COST_H12 = 3800
COST_W3 = 1050


class Sched:

    ENGS = ("sync", "gpsimd", "scalar", "vector", "tensor")

    def __init__(self):
        self.ops = []
        self.counts = {}

    SAME_ENG_WINDOW = 16

    def op(self, eng, emit, deps=(), dma_chan=None):
        chan = dma_chan if dma_chan is not None else eng
        inc = 16 if dma_chan is not None else 1
        n = self.counts.get(chan, 0) + inc
        self.counts[chan] = n
        ordn = self.counts.get("_ord_" + eng, 0) + 1
        self.counts["_ord_" + eng] = ordn
        rec = {"eng": eng, "emit": emit,
               "deps": [d for d in deps if d is not None],
               "chan": chan, "seq": n, "ord": ordn}
        self.ops.append(rec)
        return rec

    def emit_all(self, block, sems):
        per_eng = {e: [] for e in self.ENGS}
        for rec in self.ops:
            per_eng[rec["eng"]].append(rec)
        for eng_name in self.ENGS:
            recs = per_eng[eng_name]
            if not recs:
                continue

            def body(engine, recs=recs, eng_name=eng_name):
                high = {}
                for rec in recs:
                    for dep in rec["deps"]:
                        ch, val = dep["chan"], dep["seq"]
                        if ch == eng_name and dep["eng"] == eng_name:
                            if rec["ord"] - dep["ord"] > self.SAME_ENG_WINDOW:
                                continue
                        if high.get(ch, 0) >= val:
                            continue
                        engine.wait_ge(sems[ch], val)
                        high[ch] = val
                    instr = rec["emit"](engine)
                    instr.then_inc(
                        sems[rec["chan"]],
                        16 if rec["chan"] not in self.ENGS else 1)
            getattr(block, eng_name)(body)


def build_core():
    nc = bass.Bass("TRN2", target_bir_lowering=False,
                   detect_race_conditions=False,
                   dynamic_dma_scratch_size=16384)

    x_in = nc.dram_tensor("x", [S, D], F32, kind="ExternalInput")
    wq_in = nc.dram_tensor("wq", [D, D], F32, kind="ExternalInput")
    wk_in = nc.dram_tensor("wk", [D, D], F32, kind="ExternalInput")
    wv_in = nc.dram_tensor("wv", [D, D], F32, kind="ExternalInput")
    wo_in = nc.dram_tensor("wo", [D, D], F32, kind="ExternalInput")
    b_in = nc.dram_tensor("bqkv", [4, D], F32, kind="ExternalInput")
    wgt_in = nc.dram_tensor("wgt", [D, E8], F32, kind="ExternalInput")
    w1t_in = nc.dram_tensor("w1t", [E8, D, FF], F32, kind="ExternalInput")
    w2t_in = nc.dram_tensor("w2t", [E8, D, FF], F32, kind="ExternalInput")
    w3t_in = nc.dram_tensor("w3t", [E8, FF, D], F32, kind="ExternalInput")
    id_in = nc.dram_tensor("ident", [P, P], F32, kind="ExternalInput")
    out_t = nc.dram_tensor("out", [S, D], F32, kind="ExternalOutput")
    KDBG = bool(os.environ.get("KDBG_DUMP"))
    dbg = {}
    if KDBG:
        for nm, shp in (("d_xnT0", [P, S]), ("d_gates", [P, NTB * E8]),
                        ("d_qT", [P, S]), ("d_kT", [P, S]),
                        ("d_vst0", [P, H * (DH + 1)]), ("d_attnp", [P, S]),
                        ("d_acc0a", [P, D]), ("d_acc0f", [P, D]),
                        ("d_hid0", [P, S])):
            dbg[nm] = nc.dram_tensor(nm, shp, F32, kind="ExternalOutput")

    sch = Sched()
    OP = sch.op

    with ExitStack() as ctx:
        def sbt(name, shape, dt=F32):
            return ctx.enter_context(nc.sbuf_tensor(name, shape, dt))

        xn_T = [sbt(f"xnT{d}", [P, S], BF16) for d in range(NDC)]
        vstat = [sbt(f"vst{t}", [P, H * (DH + 1)], BF16) for t in range(NTB)]
        acc = [sbt(f"acc{t}", [P, D], BF16) for t in range(NTB)]
        gates = sbt("gates", [P, NTB * E8], F32)
        qT = sbt("qT", [P, S], BF16)
        kT = sbt("kT", [P, S], BF16)
        qk_tok = sbt("qk_tok", [P, P], BF16)
        attnp = sbt("attnp", [P, S], BF16)
        oT = [sbt(f"oT{i}", [P, P], BF16) for i in range(2)]
        ych = [sbt(f"ych{i}", [P, S], F32) for i in range(2)]
        xst = [ych[1][:, D:2 * D]] * 2
        wch = [sbt(f"wch{i}", [P, S], F32) for i in range(2)]
        scbf = [sbt(f"scbf{i}", [P, S], BF16) for i in range(2)]
        ncc = [sbt(f"ncc{i}", [P, 1], F32) for i in range(2)]
        ccc = [sbt(f"ccc{i}", [P, 1], F32) for i in range(2)]
        vsc = [sbt(f"vsc{i}", [P, DH + 1], BF16) for i in range(2)]
        fact = sbt("fact", [P, 1], F32)
        hid = [sbt(f"hid{f}", [P, S], BF16) for f in range(4)]
        sig_sb = sbt("sig_sb", [P, 512], BF16)
        hidtmp = sbt("hidtmp", [P, 512], BF16)
        wqb = [sbt(f"wqb{i}", [P, D], BF16) for i in range(2)]
        wkb = [sbt(f"wkb{i}", [P, D], BF16) for i in range(2)]
        wob = [sbt("wob0", [P, D], BF16)] * 2
        w1b = [sbt(f"w1b{i}", [P, D], BF16) for i in range(2)]
        w2b = [sbt(f"w2b{i}", [P, D], BF16) for i in range(2)]
        w3b = [sbt(f"w3b{i}", [P, D], BF16) for i in range(4)]
        bias_bf = sbt("bias_bf", [1, 4 * D], BF16)
        wgt_bf = sbt("wgt_bf", [P, NDC * E8], BF16)
        id_bf = sbt("id_bf", [P, P], BF16)
        ones1 = sbt("ones1", [1, P], BF16)
        xn_tok = sbt("xn_tok", [P, D], BF16)
        lnsc = [sbt(f"lnsc{t}", [P, 8], F32) for t in range(NTB)]
        lnb_t = sbt("lnb_t", [P, 1], F32)
        eps_t = sbt("eps_t", [P, 1], F32)
        tiny_t = sbt("tiny_t", [P, 1], F32)
        rq2 = sbt("rq2", [P, 2], F32)
        gsc = [wch[0][:, D + t * 40:D + (t + 1) * 40] for t in range(NTB)]

        sems = {}
        for chan in ("sync", "gpsimd", "scalar", "vector", "tensor",
                     "dxin", "c_ident", "c_bias", "c_wgt", "c_wv",
                     "c_wqk0", "c_wqk1", "c_wo", "c_w12_0", "c_w12_1",
                     "c_w3", "dsto"):
            sems[chan] = ctx.enter_context(nc.semaphore("sem_" + chan))

        block = ctx.enter_context(nc.Block())

        def dma_cast(dst_ap, src_ap):
            return lambda g: g.dma_start(dst_ap, src_ap)

        ld_ident = OP("gpsimd", dma_cast(id_bf[:], id_in.ap()), dma_chan="c_ident")
        ld_bias = OP("gpsimd", dma_cast(
            bias_bf[:], bass.AP(b_in, 0, [[4 * D, 1], [1, 4 * D]])), dma_chan="c_bias")
        ld_wgt = OP("gpsimd", dma_cast(
            wgt_bf[:].rearrange("p (c e) -> p c e", e=E8),
            wgt_in.ap().rearrange("(c p) e -> p c e", p=P)),
            dma_chan="c_wgt")

        ms_ones = OP("vector", lambda v: v.memset(ones1[:], 1.0))
        ms_lnb = OP("vector", lambda v: v.memset(lnb_t[:], LN_BIAS))
        ms_eps = OP("vector", lambda v: v.memset(eps_t[:], 1e-5))
        ms_tiny = OP("vector", lambda v: v.memset(tiny_t[:], 1e-38))
        ms_vstat = [OP("vector", lambda v, t=t: v.memset(vstat[t][:], 1.0))
                    for t in range(NTB)]
        ms_acc = [OP("vector", lambda v, t=t: v.memset(acc[t][:], 0.0))
                  for t in range(NTB)]

        pctx = ExitStack()
        pb_st = pctx.enter_context(nc.psum_tensor("pb_st", [P, 512], F32))
        pvt = [pctx.enter_context(nc.psum_tensor(f"pvt{i}", [P, 512], F32))
               for i in range(3)]
        fb0 = pctx.enter_context(nc.psum_tensor("fb0", [P, 512], F32))
        fb1 = pctx.enter_context(nc.psum_tensor("fb1", [P, 512], F32))
        pool = pctx.enter_context(nc.psum_tensor("pool", [P, 512], F32))
        pb_tr = pctx.enter_context(nc.psum_tensor("pb_tr", [P, 384], BF16))

        def pvt_unit(u):
            t = u // 7
            off = (u - t * 7) * (DH + 2)
            return pvt[t][:, off:off + DH + 1]

        prev_xn = None
        prev_tr = None
        xn_tb_done = []
        last_cp = None
        for tb in range(NTB):
            sl = tb % 2
            dx = OP("sync", lambda s, tb=tb, sl=sl: s.dma_start(
                xst[sl], x_in.ap()[tb * P:(tb + 1) * P, :]),
                deps=[prev_xn] if tb >= 1 else [], dma_chan="dxin")
            L = lnsc[tb]
            r1 = OP("vector", lambda v, sl=sl, L=L: v.reduce_sum(
                L[:, 0:1], xst[sl], axis=AX), deps=[dx, prev_xn])
            sq = OP("vector", lambda v, sl=sl: v.tensor_mul(
                ych[0][:, 0:D], xst[sl], xst[sl]), deps=[dx])
            r2 = OP("vector", lambda v, L=L: v.reduce_sum(
                L[:, 1:2], ych[0][:, 0:D], axis=AX), deps=[sq])
            m1 = OP("vector", lambda v, L=L: v.tensor_scalar(
                L[:, 2:3], L[:, 0:1], 1.0 / D, None, ALU.mult), deps=[r1])
            v1op = OP("vector", lambda v, L=L: v.tensor_scalar(
                L[:, 4:5], L[:, 1:2], 1.0 / D, None, ALU.mult),
                deps=[r2])
            em2 = OP("vector", lambda v, L=L: v.scalar_tensor_tensor(
                L[:, 3:4], L[:, 2:3], -1.0, L[:, 2:3], ALU.mult, ALU.mult),
                deps=[m1])
            vfin = OP("vector", lambda v, L=L: v.tensor_add(
                L[:, 4:5], L[:, 4:5], L[:, 3:4]), deps=[v1op, em2])
            lv = OP("scalar", lambda s, L=L: s.activation(
                L[:, 5:6], L[:, 4:5], AF.Ln, bias=eps_t[:]),
                deps=[vfin, ms_eps])
            rst = OP("scalar", lambda s, L=L: s.activation(
                L[:, 5:6], L[:, 5:6], AF.Exp, scale=-0.5), deps=[lv])
            nmr = OP("vector", lambda v, L=L: v.scalar_tensor_tensor(
                L[:, 6:7], L[:, 2:3], -1.0, L[:, 5:6], ALU.mult, ALU.mult),
                deps=[rst])
            xn0 = OP("vector", lambda v, sl=sl, L=L: v.tensor_scalar(
                ych[0][:, 0:D], xst[sl], L[:, 5:6], None, ALU.mult),
                deps=[nmr, prev_tr, r2])
            xn = OP("vector", lambda v, L=L: v.tensor_scalar(
                xn_tok[:], ych[0][:, 0:D], L[:, 6:7], None, ALU.add),
                deps=[xn0])
            prev_xn = xn
            for dc in range(NDC):
                tr = OP("tensor", lambda t, dc=dc: t.transpose(
                    pb_tr[:, 0:P], xn_tok[:, dc * P:(dc + 1) * P], id_bf[:]),
                    deps=[xn, ld_ident, last_cp])
                prev_tr = tr
                last_cp = OP("vector", lambda v, dc=dc, tb=tb: v.tensor_copy(
                    xn_T[dc][:, tb * P:(tb + 1) * P], pb_tr[:, 0:P]),
                    deps=[tr])
            xn_tb_done.append(last_cp)
        xn_all = last_cp

        gmm_last = None
        for tb in range(NTB):
            for dc in range(NDC):
                gmm_last = OP("tensor", lambda t, tb=tb, dc=dc: t.matmul(
                    pool[:, tb * E8:(tb + 1) * E8],
                    xn_T[dc][:, tb * P:(tb + 1) * P],
                    wgt_bf[:, dc * E8:(dc + 1) * E8],
                    start=(dc == 0), stop=(dc == NDC - 1)),
                    deps=[xn_tb_done[tb], ld_wgt])
        gev = OP("vector", lambda v: v.tensor_copy(wch[1][:, 0:P], pool[:, 0:P]),
                 deps=[gmm_last])
        gate_done = []
        for tb in range(NTB):
            gsl = wch[1][:, tb * E8:(tb + 1) * E8]
            G = gsc[tb]
            gv1 = G[:, 0:1]; gnv1 = G[:, 1:2]; gv2 = G[:, 2:3]
            gden = G[:, 3:4]; grd = G[:, 4:5]
            gs1 = G[:, 8:16]; gt1 = G[:, 16:24]; gt2 = G[:, 24:32]
            ge1 = G[:, 32:40]
            a1 = OP("vector", lambda v, gsl=gsl, gv1=gv1: v.reduce_max(
                gv1, gsl, axis=AX), deps=[gev])
            a2 = OP("scalar", lambda s, gsl=gsl, gs1=gs1, gv1=gv1: s.activation(
                gs1, gsl, AF.Relu, bias=gv1, scale=-1.0), deps=[a1])
            a3 = OP("scalar", lambda s, gs1=gs1: s.activation(
                gs1, gs1, AF.Sign), deps=[a2])
            a4 = OP("vector", lambda v, gsl=gsl, gt1=gt1, gs1=gs1: v.tensor_mul(
                gt1, gsl, gs1), deps=[a3])
            a5 = OP("vector", lambda v, gt2=gt2, gs1=gs1: v.tensor_scalar(
                gt2, gs1, 1e30, -1e30, ALU.mult, ALU.add), deps=[a3])
            a6 = OP("vector", lambda v, gt1=gt1, gt2=gt2: v.tensor_add(
                gt1, gt1, gt2), deps=[a4, a5])
            a7 = OP("vector", lambda v, gv2=gv2, gt1=gt1: v.reduce_max(
                gv2, gt1, axis=AX), deps=[a6])
            a8 = OP("vector", lambda v, gnv1=gnv1, gv1=gv1: v.tensor_scalar(
                gnv1, gv1, -1.0, None, ALU.mult), deps=[a1])
            a9 = OP("scalar", lambda s, gsl=gsl, ge1=ge1, gnv1=gnv1: s.activation(
                ge1, gsl, AF.Exp, bias=gnv1), deps=[a8])
            a10 = OP("scalar", lambda s, gsl=gsl, gt2=gt2, gv2=gv2: s.activation(
                gt2, gsl, AF.Relu, bias=gv2, scale=-1.0), deps=[a7])
            a11 = OP("scalar", lambda s, gt2=gt2: s.activation(
                gt2, gt2, AF.Sign), deps=[a10])
            a12 = OP("vector", lambda v, gt2=gt2, ge1=ge1: v.tensor_mul(
                gt2, ge1, gt2), deps=[a11, a9])
            a13 = OP("vector", lambda v, ge1=ge1, gt2=gt2: v.tensor_sub(
                ge1, ge1, gt2), deps=[a12])
            a14 = OP("scalar", lambda s, gden=gden, gv2=gv2, gnv1=gnv1: s.activation(
                gden, gv2, AF.Exp, bias=gnv1), deps=[a7])
            a15 = OP("vector", lambda v, gden=gden: v.tensor_scalar(
                gden, gden, 1.0, None, ALU.add), deps=[a14])
            a16 = OP("vector", lambda v, grd=grd, gden=gden: v.reciprocal(
                grd, gden), deps=[a15])
            a17 = OP("scalar", lambda s, tb=tb, ge1=ge1, grd=grd: s.activation(
                gates[:, tb * E8:(tb + 1) * E8], ge1, AF.Copy,
                scale=grd), deps=[a16, a13])
            gate_done.append(a17)
        gates_all = gate_done[-1]

        TH = {"pool": gev, "sig": None, "gm1": None, "gm2": None,
              "w12": [None, None], "w3": [None] * 4, "hidf": [None] * 4,
              "ffn_last": None, "done": False,
              "cost_left": 512 * COST_H12 + 1024 * COST_W3,
              "pops_left": 32 + 8 * 32 + 8 * 16 + 256 * 2}

        def emit_h12(e, fq, i, tq, hid_ready):
            slf = (fq * 4 + i) % 2
            if tq == 0:
                fc = fq * 4 + i
                TH["l1d"] = OP("gpsimd", dma_cast(
                    w1b[slf][:].rearrange("p (c f) -> p c f", f=P),
                    w1t_in.ap()[e, :, fc * P:(fc + 1) * P]
                    .rearrange("(c p) f -> p c f", p=P)),
                    deps=[TH["w12"][slf]], dma_chan="c_w12_" + str(slf))
                TH["l2d"] = OP("gpsimd", dma_cast(
                    w2b[slf][:].rearrange("p (c f) -> p c f", f=P),
                    w2t_in.ap()[e, :, fc * P:(fc + 1) * P]
                    .rearrange("(c p) f -> p c f", p=P)),
                    deps=[TH["w12"][slf]], dma_chan="c_w12_" + str(slf))
            mm1 = None
            for dc in range(NDC):
                dd = [TH["l1d"], xn_all]
                if dc == 0:
                    dd += [TH["sig"], TH["gm2"]]
                mm1 = OP("tensor", lambda t, slf=slf, dc=dc,
                         tq=tq: t.matmul(
                             fb0[:],
                             w1b[slf][:, dc * P:(dc + 1) * P],
                             xn_T[dc][:, tq * 512:(tq + 1) * 512],
                             start=(dc == 0),
                             stop=(dc == NDC - 1)), deps=dd)
            sig = OP("scalar", lambda s: s.activation(
                sig_sb[:], fb0[:], AF.Sigmoid),
                deps=[mm1, TH["gm1"]])
            mm2 = None
            for dc in range(NDC):
                dd = [TH["l2d"], xn_all]
                if dc == 0:
                    dd.append(TH["gm1"])
                mm2 = OP("tensor", lambda t, slf=slf, dc=dc,
                         tq=tq: t.matmul(
                             fb1[:],
                             w2b[slf][:, dc * P:(dc + 1) * P],
                             xn_T[dc][:, tq * 512:(tq + 1) * 512],
                             start=(dc == 0),
                             stop=(dc == NDC - 1)), deps=dd)
            gm1 = OP("vector", lambda v: v.tensor_mul(
                hidtmp[:], fb1[:], sig_sb[:]),
                deps=[mm2, sig, TH["gm2"]])
            gm2 = OP("vector", lambda v, i=i, tq=tq:
                     v.tensor_mul(
                         hid[i][:, tq * 512:(tq + 1) * 512],
                         hidtmp[:], fb0[:]),
                     deps=[gm1, mm1, TH["hidf"][i]])
            TH["sig"], TH["gm1"], TH["gm2"] = sig, gm1, gm2
            hid_ready[i * 4 + tq] = gm2
            if tq == 3:
                TH["w12"][slf] = gm1

        def emit_w3(e, fq, tb, dh, ld_w3, hid_ready):
            mm3 = None
            for i in range(4):
                dd = [ld_w3[i], hid_ready[i * 4 + tb // 4]]
                if i == 0:
                    dd.append(TH["pool"])
                mm3 = OP("tensor", lambda t, i=i, tb=tb, dh=dh:
                         t.matmul(
                             pool[:],
                             hid[i][:, tb * P:(tb + 1) * P],
                             w3b[i][:, dh * 512:(dh + 1) * 512],
                             start=(i == 0), stop=(i == 3)),
                         deps=dd)
            ga = OP("vector", lambda v, e=e, tb=tb, dh=dh:
                    v.scalar_tensor_tensor(
                        acc[tb][:, dh * 512:(dh + 1) * 512],
                        pool[:],
                        gates[:, tb * E8 + e:tb * E8 + e + 1],
                        acc[tb][:, dh * 512:(dh + 1) * 512],
                        ALU.mult, ALU.add),
                    deps=[mm3, gates_all, ms_acc[tb]])
            TH["pool"] = ga
            TH["ffn_last"] = ga
            if tb == NTB - 1 and dh == 1:
                for i in range(4):
                    TH["w3"][i] = ga
                    TH["hidf"][i] = ga

        def ffn_units():
            for e in range(E8):
                for fq in range(4):
                    ld_w3 = []

                    def emit_w3_loads(e=e, fq=fq, ld_w3=ld_w3):
                        for i in range(4):
                            fc = fq * 4 + i
                            ld_w3.append(OP("gpsimd", dma_cast(
                                w3b[i][:],
                                w3t_in.ap()[e, fc * P:(fc + 1) * P, :]),
                                deps=[TH["w3"][i]], dma_chan="c_w3"))
                    hid_ready = [None] * 16
                    for i in range(4):
                        for tq in range(4):
                            if i == 0 and tq == 0:
                                yield ("h12", COST_H12,
                                       lambda e=e, fq=fq, i=i, tq=tq,
                                       hid_ready=hid_ready, ew=emit_w3_loads:
                                       (ew(), emit_h12(e, fq, i, tq,
                                                       hid_ready)))
                            else:
                                yield ("h12", COST_H12,
                                       lambda e=e, fq=fq, i=i, tq=tq,
                                       hid_ready=hid_ready:
                                       emit_h12(e, fq, i, tq, hid_ready))
                    for tb in range(NTB):
                        for dh in range(2):
                            yield ("w3", COST_W3,
                                   lambda e=e, fq=fq, tb=tb, dh=dh,
                                   ld_w3=ld_w3, hid_ready=hid_ready:
                                   emit_w3(e, fq, tb, dh, ld_w3, hid_ready))

        ffn_gen = ffn_units()
        TH["pending"] = None

        def pop_ffn(budget, w3_only=False):
            while budget > 0 and not TH["done"]:
                if TH["pending"] is None:
                    try:
                        TH["pending"] = next(ffn_gen)
                    except StopIteration:
                        TH["done"] = True
                        return
                kind, cost, fn = TH["pending"]
                if w3_only and kind != "w3":
                    return
                fn()
                TH["pending"] = None
                budget -= cost

        def drain_ffn():
            pop_ffn(10 ** 12)

        prev_jh_done = None
        for jh in range(2):
            ld_wv_a = OP("gpsimd", dma_cast(
                scbf[0][:].rearrange("p (c j) -> p c j", j=512),
                wv_in.ap()[0:512, jh * 512:(jh + 1) * 512]
                .rearrange("(c p) j -> p c j", p=P)),
                deps=[prev_jh_done] if jh else [], dma_chan="c_wv")
            ld_wv_b = OP("gpsimd", dma_cast(
                scbf[1][:].rearrange("p (c j) -> p c j", j=512),
                wv_in.ap()[512:1024, jh * 512:(jh + 1) * 512]
                .rearrange("(c p) j -> p c j", p=P)),
                deps=[prev_jh_done] if jh else [], dma_chan="c_wv")
            for tb in range(NTB):
                mm = None
                for dc in range(NDC):
                    dd = [xn_tb_done[tb], ld_wv_a, ld_wv_b]
                    if dc == 0:
                        dd.append(TH["pool"])
                    mm = OP("tensor", lambda t, tb=tb, dc=dc: t.matmul(
                        pool[:],
                        xn_T[dc][:, tb * P:(tb + 1) * P],
                        (scbf[0] if dc < 4 else scbf[1])[:, (dc % 4) * 512:(dc % 4 + 1) * 512],
                        start=(dc == 0), stop=False), deps=dd)
                mmb = OP("tensor", lambda t, jh=jh: t.matmul(
                    pool[:], ones1[:],
                    bias_bf[:, 2 * D + jh * 512:2 * D + (jh + 1) * 512],
                    start=False, stop=True), deps=[ld_bias, ms_ones, mm])
                ev = None
                for hh in range(8):
                    h = jh * 8 + hh
                    ev = OP("vector", lambda v, tb=tb, h=h, hh=hh:
                            v.tensor_copy(
                                vstat[tb][:, h * (DH + 1):h * (DH + 1) + DH],
                                pool[:, hh * DH:(hh + 1) * DH]),
                            deps=[mmb, ms_vstat[tb]])
                TH["pool"] = ev
                pop_ffn(COST_H12)
            prev_jh_done = TH["pool"]
        v_all = TH["pool"]

        pair_wslot_free = [None, None]
        c1_slot = [None, None]
        vs_slot = [None, None]
        prev_pair_done = v_all
        prev_clip = None
        qk_ready = None
        prev_head_ev = None
        pv_by_kc_prev = {}
        last_wch_reader = [gates_all, gates_all]
        for pair in range(8):
            slw = pair % 2
            ld_wq = OP("gpsimd", dma_cast(
                wqb[slw][:].rearrange("p (c j) -> p c j", j=P),
                wq_in.ap()[:, pair * P:(pair + 1) * P]
                .rearrange("(c p) j -> p c j", p=P)),
                deps=[pair_wslot_free[slw]], dma_chan="c_wqk" + str(slw))
            ld_wk = OP("gpsimd", dma_cast(
                wkb[slw][:].rearrange("p (c j) -> p c j", j=P),
                wk_in.ap()[:, pair * P:(pair + 1) * P]
                .rearrange("(c p) j -> p c j", p=P)),
                deps=[pair_wslot_free[slw]], dma_chan="c_wqk" + str(slw))
            ld_wo = OP("gpsimd", dma_cast(
                wob[slw][:], wo_in.ap()[pair * P:(pair + 1) * P, :]),
                deps=[prev_pair_done if pair > 0 else None], dma_chan="c_wo")

            prev_ts = None
            for which, wblk, ldw, brow, dstT in (
                    (0, wqb[slw], ld_wq, 0, qT),
                    (1, wkb[slw], ld_wk, 1, kT)):
                for tb in range(NTB):
                    mm = None
                    for dc in range(NDC):
                        dd = [xn_tb_done[tb], ldw]
                        if dc == 0:
                            dd.append(prev_ts if prev_ts is not None
                                      else prev_pair_done)
                            dd.append(prev_clip)
                        mm = OP("tensor", lambda t, tb=tb, dc=dc, wblk=wblk:
                                t.matmul(
                                    pb_st[:, 0:P],
                                    xn_T[dc][:, tb * P:(tb + 1) * P],
                                    wblk[:, dc * P:(dc + 1) * P],
                                    start=(dc == 0), stop=False), deps=dd)
                    mmb = OP("tensor", lambda t, brow=brow, pair=pair:
                             t.matmul(
                                 pb_st[:, 0:P], ones1[:],
                                 bias_bf[:, brow * D + pair * P:
                                         brow * D + (pair + 1) * P],
                                 start=False, stop=True),
                             deps=[ld_bias, ms_ones, mm])
                    cp = OP("vector", lambda v: v.tensor_copy(
                        wch[0][:, 0:P], pb_st[:, 0:P]),
                        deps=[mmb, prev_ts, last_wch_reader[0]])
                    sq2 = OP("vector", lambda v: v.tensor_mul(
                        wch[0][:, P:2 * P], wch[0][:, 0:P], wch[0][:, 0:P]), deps=[cp])
                    rds = []
                    for hl in range(2):
                        rds.append(OP("vector", lambda v, hl=hl: v.reduce_sum(
                            rq2[:, hl:hl + 1],
                            wch[0][:, P + hl * DH:P + (hl + 1) * DH], axis=AX),
                            deps=[sq2, prev_ts]))
                    l1 = OP("scalar", lambda s: s.activation(
                        rq2[:], rq2[:], AF.Ln, bias=tiny_t[:]),
                        deps=rds + [ms_tiny])
                    l2 = OP("vector", lambda v: v.tensor_scalar(
                        rq2[:], rq2[:], L2EPS2, None, ALU.max), deps=[l1])
                    l3 = OP("scalar", lambda s: s.activation(
                        rq2[:], rq2[:], AF.Exp, scale=-0.5), deps=[l2])
                    tss = []
                    for hl in range(2):
                        tss.append(OP("vector", lambda v, hl=hl:
                                      v.tensor_scalar(
                                          qk_tok[:, hl * DH:(hl + 1) * DH],
                                          wch[0][:, hl * DH:(hl + 1) * DH],
                                          rq2[:, hl:hl + 1], None, ALU.mult),
                                      deps=[l3]))
                    prev_ts = tss[-1]
                    tr = OP("tensor", lambda t: t.transpose(
                        pb_tr[:, P:2 * P], qk_tok[:], id_bf[:]),
                        deps=tss + [ld_ident, qk_ready])
                    qk_ready = OP("vector", lambda v, tb=tb, dstT=dstT:
                                  v.tensor_copy(
                                      dstT[:, tb * P:(tb + 1) * P],
                                      pb_tr[:, P:2 * P]), deps=[tr])
                    pop_ffn(2 * COST_W3, w3_only=True)
            qk_all = qk_ready
            pair_wslot_free[slw] = qk_all

            pair_ev = [None, None]
            prev_score_done = {}
            for hl in range(2):
                h = 2 * pair + hl
                head_start_dep = prev_head_ev
                score_done = {}
                pv_by_kc = {}
                clips_done = {}

                def emit_qk_seg(j, seg, hl=hl):
                    nonlocal prev_clip
                    mm = OP("tensor", lambda t, hl=hl, j=j, seg=seg:
                            t.matmul(
                                pb_st[:],
                                kT[hl * DH:(hl + 1) * DH,
                                   j * P:(j + 1) * P],
                                qT[hl * DH:(hl + 1) * DH,
                                   seg * 512:(seg + 1) * 512],
                                start=True, stop=True),
                            deps=[qk_all, prev_clip])
                    dep_slot = []
                    if seg == 0:
                        if (j - 2) in score_done:
                            dep_slot = [score_done[j - 2]]
                        elif (NKC + j - 2) in prev_score_done:
                            dep_slot = [prev_score_done[NKC + j - 2]]
                    prev_clip = OP("vector", lambda v, j=j, seg=seg:
                                   v.tensor_scalar(
                                       ych[j % 2][:,
                                                  seg * 512:(seg + 1) * 512],
                                       pb_st[:], -C_CLIP, C_CLIP,
                                       ALU.max, ALU.min),
                                   deps=[mm] + dep_slot)
                    clips_done[j] = prev_clip

                for seg in range(4):
                    emit_qk_seg(0, seg)
                    pop_ffn(COST_W3, w3_only=True)
                for kc in range(NKC):
                    slc = kc % 2
                    g1 = OP("gpsimd", lambda g, slc=slc: g.tensor_tensor(
                        wch[slc][:], ych[slc][:], ych[slc][:], ALU.mult),
                        deps=[clips_done[kc], last_wch_reader[slc]]
                        if kc < 2 else [clips_done[kc]])
                    s1 = OP("scalar", lambda s, slc=slc: s.activation(
                        wch[slc][:], wch[slc][:], AF.Ln, bias=1.0,
                        scale=-1.0), deps=[g1])
                    s2 = OP("scalar", lambda s, slc=slc: s.activation(
                        wch[slc][:], wch[slc][:], AF.Sigmoid, scale=-0.5),
                        deps=[s1])
                    v2 = OP("gpsimd", lambda g, slc=slc: g.tensor_tensor(
                        ych[slc][:], ych[slc][:], wch[slc][:], ALU.mult),
                        deps=[s2])
                    s3 = OP("scalar", lambda s, slc=slc: s.activation(
                        ych[slc][:], ych[slc][:], AF.Arctan), deps=[v2])
                    pop_ffn(3 * COST_H12)
                    s4 = OP("scalar", lambda s, slc=slc: s.activation(
                        ych[slc][:], ych[slc][:], AF.Ln, bias=lnb_t[:],
                        scale=-2.0), deps=[s3, ms_lnb])
                    dep_pv_slot = ([pv_by_kc[kc - 2]] if (kc - 2) in pv_by_kc
                                   else ([pv_by_kc_prev.get(NKC - 2 + kc)]
                                         if (NKC - 2 + kc) in pv_by_kc_prev
                                         else []))
                    sdep = [s4] + dep_pv_slot + [c1_slot[slc]]
                    if pair == 0 and hl == 0 and kc < 2:
                        sdep.append(v_all)
                    s5 = OP("scalar", lambda s, slc=slc: s.activation(
                        scbf[slc][:], ych[slc][:], AF.Exp, scale=-65.0,
                        accum_out=ncc[slc][:]), deps=sdep)
                    score_done[kc] = s5
                    c1 = OP("scalar", lambda s, slc=slc: s.activation(
                        ccc[slc][:], ncc[slc][:], AF.Ln, bias=tiny_t[:],
                        scale=float(2.0 ** 72)),
                        deps=[s5, ms_tiny, vs_slot[slc]])
                    c1_slot[slc] = c1
                    c2 = OP("scalar", lambda s, slc=slc: s.activation(
                        ccc[slc][:], ccc[slc][:], AF.Exp, scale=-0.5),
                        deps=[c1])
                    vs = OP("vector", lambda v, slc=slc, kc=kc, h=h:
                            v.tensor_scalar(
                                vsc[slc][:],
                                vstat[kc][:,
                                          h * (DH + 1):(h + 1) * (DH + 1)],
                                ccc[slc][:], None, ALU.mult),
                            deps=[c2, v_all] + dep_pv_slot)
                    vs_slot[slc] = vs
                    if kc + 1 < NKC:
                        for seg in range(4):
                            emit_qk_seg(kc + 1, seg)
                            pop_ffn(COST_W3, w3_only=True)
                    pv_last = None
                    for u in range(16):
                        dd = [vs, s5]
                        if kc == 0:
                            dd.append(head_start_dep)
                        st_flag = (kc == 0 and u in (0, 7, 14))
                        sp_flag = (kc == NKC - 1 and u in (6, 13, 15))
                        pv_last = OP("tensor", lambda t, slc=slc, u=u, kc=kc,
                                     st_flag=st_flag, sp_flag=sp_flag:
                                     t.matmul(
                                         pvt_unit(u),
                                         scbf[slc][:, u * P:(u + 1) * P],
                                         vsc[slc][:],
                                         start=st_flag, stop=sp_flag,
                                         skip_group_check=True), deps=dd)
                    pv_by_kc[kc] = pv_last
                    last_wch_reader[slc] = v2
                    pop_ffn(4 * COST_W3, w3_only=True)
                pv_by_kc_prev = pv_by_kc
                prev_score_done = score_done
                ev_last = None
                for u in range(16):
                    fr = OP("vector", lambda v, u=u: v.reciprocal(
                        fact[:], pvt_unit(u)[:, DH:DH + 1]),
                        deps=[pv_by_kc[NKC - 1], ev_last])
                    ev_last = OP("vector", lambda v, u=u, hl=hl:
                                 v.tensor_scalar(
                                     attnp[:, u * P + hl * DH:
                                           u * P + (hl + 1) * DH],
                                     pvt_unit(u)[:, 0:DH], fact[:],
                                     None, ALU.mult), deps=[fr])
                pair_ev[hl] = ev_last
                prev_head_ev = ev_last

            prev_cpo = None
            last_ad = None
            for u in range(NTB):
                tro = OP("tensor", lambda t, u=u: t.transpose(
                    pb_tr[:, 2 * P:3 * P], attnp[:, u * P:(u + 1) * P],
                    id_bf[:]),
                    deps=[pair_ev[0], pair_ev[1], ld_ident, prev_cpo])
                prev_cpo = OP("vector", lambda v, u=u: v.tensor_copy(
                    oT[u % 2][:], pb_tr[:, 2 * P:3 * P]), deps=[tro])
                for dh in range(2):
                    mo = OP("tensor", lambda t, u=u, dh=dh, slw=slw,
                            pair=pair: t.matmul(
                                pool[:], oT[u % 2][:],
                                wob[slw][:, dh * 512:(dh + 1) * 512],
                                start=True, stop=(pair != 0)),
                            deps=[prev_cpo, ld_wo, TH["pool"]])
                    if pair == 0:
                        mo = OP("tensor", lambda t, dh=dh: t.matmul(
                            pool[:], ones1[:],
                            bias_bf[:, 3 * D + dh * 512:3 * D + (dh + 1) * 512],
                            start=False, stop=True), deps=[mo, ld_bias])
                    ad = OP("vector", lambda v, u=u, dh=dh:
                            v.scalar_tensor_tensor(
                                acc[u][:, dh * 512:(dh + 1) * 512],
                                pool[:], 1.0,
                                acc[u][:, dh * 512:(dh + 1) * 512],
                                ALU.mult, ALU.add),
                            deps=[mo, ms_acc[u]])
                    TH["pool"] = ad
                    last_ad = ad
                pop_ffn(2 * COST_W3, w3_only=True)
            prev_pair_done = last_ad
        attn_all = prev_pair_done

        drain_ffn()
        ffn_all = TH["ffn_last"]

        if KDBG:
            def dump(nm, ap_src, dep):
                return OP("gpsimd", lambda sy, nm=nm, ap_src=ap_src:
                          sy.dma_start(dbg[nm].ap(), ap_src), deps=[dep],
                          dma_chan="dsto")
            dump("d_xnT0", xn_T[0][:], xn_all)
            dump("d_gates", gates[:], gates_all)
            dump("d_qT", qT[:], attn_all)
            dump("d_kT", kT[:], attn_all)
            dump("d_vst0", vstat[0][:], attn_all)
            dump("d_attnp", attnp[:], attn_all)
            dump("d_hid0", hid[0][:], ffn_all)
            OP("gpsimd", lambda g: g.dma_start(dbg["d_acc0f"].ap(), acc[0][:]),
               deps=[attn_all, ffn_all], dma_chan="dsto")
            attn_all = dump("d_acc0a", acc[0][:], attn_all)

        last_st = None
        for tb in range(NTB):
            sl = 0
            dx = OP("sync", lambda s, tb=tb, sl=sl: s.dma_start(
                xst[sl], x_in.ap()[tb * P:(tb + 1) * P, :]),
                deps=[last_st, ffn_all, attn_all], dma_chan="dxin")
            ao = OP("vector", lambda v, tb=tb, sl=sl: v.tensor_add(
                xst[sl], xst[sl], acc[tb][:]), deps=[dx, ffn_all, attn_all] +
                ([last_st] if last_st else []))
            st = OP("sync", lambda s, tb=tb, sl=sl: s.dma_start(
                out_t.ap()[tb * P:(tb + 1) * P, :], xst[sl]),
                deps=[ao], dma_chan="dsto")
            last_st = st

        sch.emit_all(block, sems)

        @block.sync
        def _(sync):
            sync.wait_ge(sems["dsto"], sch.counts["dsto"])

        pctx.close()

    return nc


_NC_CACHE = {}


def kernel(L0_attn_x, L0_attn_wq, L0_attn_bq, L0_attn_wk, L0_attn_bk,
           L0_attn_wv, L0_attn_bv, L0_attn_wo, L0_attn_bo,
           L0_ffn_Wg, L0_ffn_W1, L0_ffn_W2, L0_ffn_W3):
    x = np.ascontiguousarray(np.asarray(L0_attn_x, dtype=np.float32))
    wq = np.ascontiguousarray(np.asarray(L0_attn_wq, dtype=np.float32))
    wk = np.ascontiguousarray(np.asarray(L0_attn_wk, dtype=np.float32))
    wv = np.ascontiguousarray(np.asarray(L0_attn_wv, dtype=np.float32))
    wo = np.ascontiguousarray(np.asarray(L0_attn_wo, dtype=np.float32))
    bqkv = np.ascontiguousarray(np.stack([
        np.asarray(L0_attn_bq, dtype=np.float32),
        np.asarray(L0_attn_bk, dtype=np.float32),
        np.asarray(L0_attn_bv, dtype=np.float32),
        np.asarray(L0_attn_bo, dtype=np.float32)], axis=0))
    wgt = np.ascontiguousarray(np.asarray(L0_ffn_Wg, dtype=np.float32).T)
    w1t = np.ascontiguousarray(
        np.asarray(L0_ffn_W1, dtype=np.float32).transpose(0, 2, 1))
    w2t = np.ascontiguousarray(
        np.asarray(L0_ffn_W2, dtype=np.float32).transpose(0, 2, 1))
    w3t = np.ascontiguousarray(
        np.asarray(L0_ffn_W3, dtype=np.float32).transpose(0, 2, 1))
    ident = np.eye(P, dtype=np.float32)

    if "nc" not in _NC_CACHE:
        _NC_CACHE["nc"] = build_core()
    nc = _NC_CACHE["nc"]

    in_maps = []
    for c in range(8):
        in_maps.append({
            "x": np.ascontiguousarray(x[c]),
            "wq": wq, "wk": wk, "wv": wv, "wo": wo,
            "bqkv": bqkv, "wgt": wgt,
            "w1t": w1t, "w2t": w2t, "w3t": w3t,
            "ident": ident,
        })
    res = run_bass_kernel_spmd(nc, in_maps, core_ids=list(range(8)))
    out = np.stack([res.results[c]["out"] for c in range(8)], axis=0)
    return out.astype(np.float32)


# version 08
# speedup vs baseline: 1.0152x; 1.0152x over previous
import math
import os
from contextlib import ExitStack

import numpy as np
import ml_dtypes

import concourse.bass as bass
import concourse.mybir as mybir
from concourse.bass_utils import run_bass_kernel_spmd

F32 = mybir.dt.float32
BF16 = mybir.dt.bfloat16
AF = mybir.ActivationFunctionType
ALU = mybir.AluOpType
AX = mybir.AxisListType.X

P = 128
S = 2048
D = 1024
H = 16
DH = 64
E8 = 8
FF = 2048
NTB = S // P
NDC = D // P
NKC = S // P
C_CLIP = float(np.float32(1.0 - 1e-6))
LN_BIAS = float(1.0 + math.pi / 2.0)
L2EPS2 = float(2.0 * math.log(1e-12))
COST_H12 = 3800
COST_W3 = 1050


class Sched:

    ENGS = ("sync", "gpsimd", "scalar", "vector", "tensor")

    def __init__(self):
        self.ops = []
        self.counts = {}

    SAME_ENG_WINDOW = 16

    def op(self, eng, emit, deps=(), dma_chan=None):
        chan = dma_chan if dma_chan is not None else eng
        inc = 16 if dma_chan is not None else 1
        n = self.counts.get(chan, 0) + inc
        self.counts[chan] = n
        ordn = self.counts.get("_ord_" + eng, 0) + 1
        self.counts["_ord_" + eng] = ordn
        rec = {"eng": eng, "emit": emit,
               "deps": [d for d in deps if d is not None],
               "chan": chan, "seq": n, "ord": ordn}
        self.ops.append(rec)
        return rec

    def emit_all(self, block, sems):
        per_eng = {e: [] for e in self.ENGS}
        for rec in self.ops:
            per_eng[rec["eng"]].append(rec)
        for eng_name in self.ENGS:
            recs = per_eng[eng_name]
            if not recs:
                continue

            def body(engine, recs=recs, eng_name=eng_name):
                high = {}
                for rec in recs:
                    for dep in rec["deps"]:
                        ch, val = dep["chan"], dep["seq"]
                        if ch == eng_name and dep["eng"] == eng_name:
                            if rec["ord"] - dep["ord"] > self.SAME_ENG_WINDOW:
                                continue
                        if high.get(ch, 0) >= val:
                            continue
                        engine.wait_ge(sems[ch], val)
                        high[ch] = val
                    instr = rec["emit"](engine)
                    instr.then_inc(
                        sems[rec["chan"]],
                        16 if rec["chan"] not in self.ENGS else 1)
            getattr(block, eng_name)(body)


def build_core():
    nc = bass.Bass("TRN2", target_bir_lowering=False,
                   detect_race_conditions=False,
                   dynamic_dma_scratch_size=16384)

    x_in = nc.dram_tensor("x", [S, D], F32, kind="ExternalInput")
    wq_in = nc.dram_tensor("wq", [D, D], F32, kind="ExternalInput")
    wk_in = nc.dram_tensor("wk", [D, D], F32, kind="ExternalInput")
    wv_in = nc.dram_tensor("wv", [D, D], F32, kind="ExternalInput")
    wo_in = nc.dram_tensor("wo", [D, D], F32, kind="ExternalInput")
    b_in = nc.dram_tensor("bqkv", [4, D], F32, kind="ExternalInput")
    wgt_in = nc.dram_tensor("wgt", [D, E8], F32, kind="ExternalInput")
    w1t_in = nc.dram_tensor("w1t", [E8, D, FF], F32, kind="ExternalInput")
    w2t_in = nc.dram_tensor("w2t", [E8, D, FF], F32, kind="ExternalInput")
    w3t_in = nc.dram_tensor("w3t", [E8, FF, D], F32, kind="ExternalInput")
    id_in = nc.dram_tensor("ident", [P, P], F32, kind="ExternalInput")
    out_t = nc.dram_tensor("out", [S, D], F32, kind="ExternalOutput")
    KDBG = bool(os.environ.get("KDBG_DUMP"))
    dbg = {}
    if KDBG:
        for nm, shp in (("d_xnT0", [P, S]), ("d_gates", [P, NTB * E8]),
                        ("d_qT", [P, S]), ("d_kT", [P, S]),
                        ("d_vst0", [P, H * (DH + 1)]), ("d_attnp", [P, S]),
                        ("d_acc0a", [P, D]), ("d_acc0f", [P, D]),
                        ("d_hid0", [P, S])):
            dbg[nm] = nc.dram_tensor(nm, shp, F32, kind="ExternalOutput")

    sch = Sched()
    OP = sch.op

    with ExitStack() as ctx:
        def sbt(name, shape, dt=F32):
            return ctx.enter_context(nc.sbuf_tensor(name, shape, dt))

        xn_T = [sbt(f"xnT{d}", [P, S], BF16) for d in range(NDC)]
        vstat = [sbt(f"vst{t}", [P, H * (DH + 1)], BF16) for t in range(NTB)]
        acc = [sbt(f"acc{t}", [P, D], BF16) for t in range(NTB)]
        gates = sbt("gates", [P, NTB * E8], F32)
        qT = sbt("qT", [P, S], BF16)
        kT = sbt("kT", [P, S], BF16)
        qk_tok = sbt("qk_tok", [P, P], BF16)
        attnp = sbt("attnp", [P, S], BF16)
        oT = [sbt(f"oT{i}", [P, P], BF16) for i in range(2)]
        ych = [sbt(f"ych{i}", [P, S], F32) for i in range(2)]
        xst = [ych[1][:, D:2 * D]] * 2
        wch = [sbt(f"wch{i}", [P, S], F32) for i in range(2)]
        scbf = [sbt(f"scbf{i}", [P, S], BF16) for i in range(2)]
        ncc = [sbt(f"ncc{i}", [P, 1], F32) for i in range(2)]
        ccc = [sbt(f"ccc{i}", [P, 1], F32) for i in range(2)]
        vsc = [sbt(f"vsc{i}", [P, DH + 1], BF16) for i in range(2)]
        fact = sbt("fact", [P, 1], F32)
        hid = [sbt(f"hid{f}", [P, S], BF16) for f in range(4)]
        sig_sb = sbt("sig_sb", [P, 512], BF16)
        hidtmp = sbt("hidtmp", [P, 512], BF16)
        wqb = [sbt(f"wqb{i}", [P, D], BF16) for i in range(2)]
        wkb = [sbt(f"wkb{i}", [P, D], BF16) for i in range(2)]
        wob = [sbt("wob0", [P, D], BF16)] * 2
        w1b = [sbt(f"w1b{i}", [P, D], BF16) for i in range(2)]
        w2b = [sbt(f"w2b{i}", [P, D], BF16) for i in range(2)]
        w3b = [sbt(f"w3b{i}", [P, D], BF16) for i in range(4)]
        bias_bf = sbt("bias_bf", [1, 4 * D], BF16)
        wgt_bf = sbt("wgt_bf", [P, NDC * E8], BF16)
        id_bf = sbt("id_bf", [P, P], BF16)
        ones1 = sbt("ones1", [1, P], BF16)
        xn_tok = sbt("xn_tok", [P, D], BF16)
        lnsc = [sbt(f"lnsc{t}", [P, 8], F32) for t in range(NTB)]
        lnb_t = sbt("lnb_t", [P, 1], F32)
        eps_t = sbt("eps_t", [P, 1], F32)
        tiny_t = sbt("tiny_t", [P, 1], F32)
        rq2 = sbt("rq2", [P, 2], F32)
        gsc = [wch[0][:, D + t * 40:D + (t + 1) * 40] for t in range(NTB)]

        sems = {}
        for chan in ("sync", "gpsimd", "scalar", "vector", "tensor",
                     "dxin", "c_ident", "c_bias", "c_wgt", "c_wv",
                     "c_wqk0", "c_wqk1", "c_wo", "c_w12_0", "c_w12_1",
                     "c_w3", "dsto"):
            sems[chan] = ctx.enter_context(nc.semaphore("sem_" + chan))

        block = ctx.enter_context(nc.Block())

        def dma_cast(dst_ap, src_ap):
            return lambda g: g.dma_start(dst_ap, src_ap)

        ld_ident = OP("gpsimd", dma_cast(id_bf[:], id_in.ap()), dma_chan="c_ident")
        ld_bias = OP("gpsimd", dma_cast(
            bias_bf[:], bass.AP(b_in, 0, [[4 * D, 1], [1, 4 * D]])), dma_chan="c_bias")
        ld_wgt = OP("gpsimd", dma_cast(
            wgt_bf[:].rearrange("p (c e) -> p c e", e=E8),
            wgt_in.ap().rearrange("(c p) e -> p c e", p=P)),
            dma_chan="c_wgt")

        ms_ones = OP("vector", lambda v: v.memset(ones1[:], 1.0))
        ms_lnb = OP("vector", lambda v: v.memset(lnb_t[:], LN_BIAS))
        ms_eps = OP("vector", lambda v: v.memset(eps_t[:], 1e-5))
        ms_tiny = OP("vector", lambda v: v.memset(tiny_t[:], 1e-38))
        ms_vstat = [OP("vector", lambda v, t=t: v.memset(vstat[t][:], 1.0))
                    for t in range(NTB)]
        ms_acc = [OP("vector", lambda v, t=t: v.memset(acc[t][:], 0.0))
                  for t in range(NTB)]

        pctx = ExitStack()
        pb_st = pctx.enter_context(nc.psum_tensor("pb_st", [P, 512], F32))
        pvt = [pctx.enter_context(nc.psum_tensor(f"pvt{i}", [P, 512], F32))
               for i in range(3)]
        fb0 = pctx.enter_context(nc.psum_tensor("fb0", [P, 512], F32))
        fb1 = pctx.enter_context(nc.psum_tensor("fb1", [P, 512], F32))
        pool = pctx.enter_context(nc.psum_tensor("pool", [P, 512], F32))
        pb_tr = pctx.enter_context(nc.psum_tensor("pb_tr", [P, 384], BF16))

        def pvt_unit(u):
            t = u // 7
            off = (u - t * 7) * (DH + 2)
            return pvt[t][:, off:off + DH + 1]

        prev_xn = None
        prev_tr = None
        xn_tb_done = []
        last_cp = None
        for tb in range(NTB):
            sl = tb % 2
            dx = OP("sync", lambda s, tb=tb, sl=sl: s.dma_start(
                xst[sl], x_in.ap()[tb * P:(tb + 1) * P, :]),
                deps=[prev_xn] if tb >= 1 else [], dma_chan="dxin")
            L = lnsc[tb]
            r1 = OP("vector", lambda v, sl=sl, L=L: v.reduce_sum(
                L[:, 0:1], xst[sl], axis=AX), deps=[dx, prev_xn])
            sq = OP("vector", lambda v, sl=sl: v.tensor_mul(
                ych[0][:, 0:D], xst[sl], xst[sl]), deps=[dx])
            r2 = OP("vector", lambda v, L=L: v.reduce_sum(
                L[:, 1:2], ych[0][:, 0:D], axis=AX), deps=[sq])
            m1 = OP("vector", lambda v, L=L: v.tensor_scalar(
                L[:, 2:3], L[:, 0:1], 1.0 / D, None, ALU.mult), deps=[r1])
            v1op = OP("vector", lambda v, L=L: v.tensor_scalar(
                L[:, 4:5], L[:, 1:2], 1.0 / D, None, ALU.mult),
                deps=[r2])
            em2 = OP("vector", lambda v, L=L: v.scalar_tensor_tensor(
                L[:, 3:4], L[:, 2:3], -1.0, L[:, 2:3], ALU.mult, ALU.mult),
                deps=[m1])
            vfin = OP("vector", lambda v, L=L: v.tensor_add(
                L[:, 4:5], L[:, 4:5], L[:, 3:4]), deps=[v1op, em2])
            lv = OP("scalar", lambda s, L=L: s.activation(
                L[:, 5:6], L[:, 4:5], AF.Ln, bias=eps_t[:]),
                deps=[vfin, ms_eps])
            rst = OP("scalar", lambda s, L=L: s.activation(
                L[:, 5:6], L[:, 5:6], AF.Exp, scale=-0.5), deps=[lv])
            nmr = OP("vector", lambda v, L=L: v.scalar_tensor_tensor(
                L[:, 6:7], L[:, 2:3], -1.0, L[:, 5:6], ALU.mult, ALU.mult),
                deps=[rst])
            xn0 = OP("vector", lambda v, sl=sl, L=L: v.tensor_scalar(
                ych[0][:, 0:D], xst[sl], L[:, 5:6], None, ALU.mult),
                deps=[nmr, prev_tr, r2])
            xn = OP("vector", lambda v, L=L: v.tensor_scalar(
                xn_tok[:], ych[0][:, 0:D], L[:, 6:7], None, ALU.add),
                deps=[xn0])
            prev_xn = xn
            for dc in range(NDC):
                tr = OP("tensor", lambda t, dc=dc: t.transpose(
                    pb_tr[:, 0:P], xn_tok[:, dc * P:(dc + 1) * P], id_bf[:]),
                    deps=[xn, ld_ident, last_cp])
                prev_tr = tr
                last_cp = OP("vector", lambda v, dc=dc, tb=tb: v.tensor_copy(
                    xn_T[dc][:, tb * P:(tb + 1) * P], pb_tr[:, 0:P]),
                    deps=[tr])
            xn_tb_done.append(last_cp)
        xn_all = last_cp

        gmm_last = None
        for tb in range(NTB):
            for dc in range(NDC):
                gmm_last = OP("tensor", lambda t, tb=tb, dc=dc: t.matmul(
                    pool[:, tb * E8:(tb + 1) * E8],
                    xn_T[dc][:, tb * P:(tb + 1) * P],
                    wgt_bf[:, dc * E8:(dc + 1) * E8],
                    start=(dc == 0), stop=(dc == NDC - 1)),
                    deps=[xn_tb_done[tb], ld_wgt])
        gev = OP("vector", lambda v: v.tensor_copy(wch[1][:, 0:P], pool[:, 0:P]),
                 deps=[gmm_last])
        gate_done = []
        for tb in range(NTB):
            gsl = wch[1][:, tb * E8:(tb + 1) * E8]
            G = gsc[tb]
            gv1 = G[:, 0:1]; gnv1 = G[:, 1:2]; gv2 = G[:, 2:3]
            gden = G[:, 3:4]; grd = G[:, 4:5]
            gs1 = G[:, 8:16]; gt1 = G[:, 16:24]; gt2 = G[:, 24:32]
            ge1 = G[:, 32:40]
            a1 = OP("vector", lambda v, gsl=gsl, gv1=gv1: v.reduce_max(
                gv1, gsl, axis=AX), deps=[gev])
            a2 = OP("scalar", lambda s, gsl=gsl, gs1=gs1, gv1=gv1: s.activation(
                gs1, gsl, AF.Relu, bias=gv1, scale=-1.0), deps=[a1])
            a3 = OP("scalar", lambda s, gs1=gs1: s.activation(
                gs1, gs1, AF.Sign), deps=[a2])
            a4 = OP("vector", lambda v, gsl=gsl, gt1=gt1, gs1=gs1: v.tensor_mul(
                gt1, gsl, gs1), deps=[a3])
            a5 = OP("vector", lambda v, gt2=gt2, gs1=gs1: v.tensor_scalar(
                gt2, gs1, 1e30, -1e30, ALU.mult, ALU.add), deps=[a3])
            a6 = OP("vector", lambda v, gt1=gt1, gt2=gt2: v.tensor_add(
                gt1, gt1, gt2), deps=[a4, a5])
            a7 = OP("vector", lambda v, gv2=gv2, gt1=gt1: v.reduce_max(
                gv2, gt1, axis=AX), deps=[a6])
            a8 = OP("vector", lambda v, gnv1=gnv1, gv1=gv1: v.tensor_scalar(
                gnv1, gv1, -1.0, None, ALU.mult), deps=[a1])
            a9 = OP("scalar", lambda s, gsl=gsl, ge1=ge1, gnv1=gnv1: s.activation(
                ge1, gsl, AF.Exp, bias=gnv1), deps=[a8])
            a10 = OP("scalar", lambda s, gsl=gsl, gt2=gt2, gv2=gv2: s.activation(
                gt2, gsl, AF.Relu, bias=gv2, scale=-1.0), deps=[a7])
            a11 = OP("scalar", lambda s, gt2=gt2: s.activation(
                gt2, gt2, AF.Sign), deps=[a10])
            a12 = OP("vector", lambda v, gt2=gt2, ge1=ge1: v.tensor_mul(
                gt2, ge1, gt2), deps=[a11, a9])
            a13 = OP("vector", lambda v, ge1=ge1, gt2=gt2: v.tensor_sub(
                ge1, ge1, gt2), deps=[a12])
            a14 = OP("scalar", lambda s, gden=gden, gv2=gv2, gnv1=gnv1: s.activation(
                gden, gv2, AF.Exp, bias=gnv1), deps=[a7])
            a15 = OP("vector", lambda v, gden=gden: v.tensor_scalar(
                gden, gden, 1.0, None, ALU.add), deps=[a14])
            a16 = OP("vector", lambda v, grd=grd, gden=gden: v.reciprocal(
                grd, gden), deps=[a15])
            a17 = OP("scalar", lambda s, tb=tb, ge1=ge1, grd=grd: s.activation(
                gates[:, tb * E8:(tb + 1) * E8], ge1, AF.Copy,
                scale=grd), deps=[a16, a13])
            gate_done.append(a17)
        gates_all = gate_done[-1]

        TH = {"pool": gev, "sig": None, "gm1": None, "gm2": None,
              "w12": [None, None], "w3": [None] * 4, "hidf": [None] * 4,
              "ffn_last": None, "done": False,
              "cost_left": 512 * COST_H12 + 1024 * COST_W3,
              "pops_left": 32 + 8 * 32 + 8 * 16 + 256 * 2}

        def emit_h12(e, fq, i, tq, hid_ready):
            slf = (fq * 4 + i) % 2
            if tq == 0:
                fc = fq * 4 + i
                TH["l1d"] = OP("gpsimd", dma_cast(
                    w1b[slf][:].rearrange("p (c f) -> p c f", f=P),
                    w1t_in.ap()[e, :, fc * P:(fc + 1) * P]
                    .rearrange("(c p) f -> p c f", p=P)),
                    deps=[TH["w12"][slf]], dma_chan="c_w12_" + str(slf))
                TH["l2d"] = OP("gpsimd", dma_cast(
                    w2b[slf][:].rearrange("p (c f) -> p c f", f=P),
                    w2t_in.ap()[e, :, fc * P:(fc + 1) * P]
                    .rearrange("(c p) f -> p c f", p=P)),
                    deps=[TH["w12"][slf]], dma_chan="c_w12_" + str(slf))
            mm1 = None
            for dc in range(NDC):
                dd = [TH["l1d"], xn_all]
                if dc == 0:
                    dd += [TH["sig"], TH["gm2"]]
                mm1 = OP("tensor", lambda t, slf=slf, dc=dc,
                         tq=tq: t.matmul(
                             fb0[:],
                             w1b[slf][:, dc * P:(dc + 1) * P],
                             xn_T[dc][:, tq * 512:(tq + 1) * 512],
                             start=(dc == 0),
                             stop=(dc == NDC - 1)), deps=dd)
            sig = OP("scalar", lambda s: s.activation(
                sig_sb[:], fb0[:], AF.Sigmoid),
                deps=[mm1, TH["gm1"]])
            mm2 = None
            for dc in range(NDC):
                dd = [TH["l2d"], xn_all]
                if dc == 0:
                    dd.append(TH["gm1"])
                mm2 = OP("tensor", lambda t, slf=slf, dc=dc,
                         tq=tq: t.matmul(
                             fb1[:],
                             w2b[slf][:, dc * P:(dc + 1) * P],
                             xn_T[dc][:, tq * 512:(tq + 1) * 512],
                             start=(dc == 0),
                             stop=(dc == NDC - 1)), deps=dd)
            gm1 = OP("vector", lambda v: v.tensor_mul(
                hidtmp[:], fb1[:], sig_sb[:]),
                deps=[mm2, sig, TH["gm2"]])
            gm2 = OP("vector", lambda v, i=i, tq=tq:
                     v.tensor_mul(
                         hid[i][:, tq * 512:(tq + 1) * 512],
                         hidtmp[:], fb0[:]),
                     deps=[gm1, mm1, TH["hidf"][i]])
            TH["sig"], TH["gm1"], TH["gm2"] = sig, gm1, gm2
            hid_ready[i * 4 + tq] = gm2
            if tq == 3:
                TH["w12"][slf] = gm1

        def emit_w3(e, fq, tb, dh, ld_w3, hid_ready):
            mm3 = None
            for i in range(4):
                dd = [ld_w3[i], hid_ready[i * 4 + tb // 4]]
                if i == 0:
                    dd.append(TH["pool"])
                mm3 = OP("tensor", lambda t, i=i, tb=tb, dh=dh:
                         t.matmul(
                             pool[:],
                             hid[i][:, tb * P:(tb + 1) * P],
                             w3b[i][:, dh * 512:(dh + 1) * 512],
                             start=(i == 0), stop=(i == 3)),
                         deps=dd)
            ga = OP("vector", lambda v, e=e, tb=tb, dh=dh:
                    v.scalar_tensor_tensor(
                        acc[tb][:, dh * 512:(dh + 1) * 512],
                        pool[:],
                        gates[:, tb * E8 + e:tb * E8 + e + 1],
                        acc[tb][:, dh * 512:(dh + 1) * 512],
                        ALU.mult, ALU.add),
                    deps=[mm3, gates_all, ms_acc[tb]])
            TH["pool"] = ga
            TH["ffn_last"] = ga
            if tb == NTB - 1 and dh == 1:
                for i in range(4):
                    TH["w3"][i] = ga
                    TH["hidf"][i] = ga

        def ffn_units():
            for e in range(E8):
                for fq in range(4):
                    ld_w3 = []

                    def emit_w3_loads(e=e, fq=fq, ld_w3=ld_w3):
                        for i in range(4):
                            fc = fq * 4 + i
                            ld_w3.append(OP("gpsimd", dma_cast(
                                w3b[i][:],
                                w3t_in.ap()[e, fc * P:(fc + 1) * P, :]),
                                deps=[TH["w3"][i]], dma_chan="c_w3"))
                    hid_ready = [None] * 16
                    for i in range(4):
                        for tq in range(4):
                            if i == 0 and tq == 0:
                                yield ("h12", COST_H12,
                                       lambda e=e, fq=fq, i=i, tq=tq,
                                       hid_ready=hid_ready, ew=emit_w3_loads:
                                       (ew(), emit_h12(e, fq, i, tq,
                                                       hid_ready)))
                            else:
                                yield ("h12", COST_H12,
                                       lambda e=e, fq=fq, i=i, tq=tq,
                                       hid_ready=hid_ready:
                                       emit_h12(e, fq, i, tq, hid_ready))
                    for tb in range(NTB):
                        for dh in range(2):
                            yield ("w3", COST_W3,
                                   lambda e=e, fq=fq, tb=tb, dh=dh,
                                   ld_w3=ld_w3, hid_ready=hid_ready:
                                   emit_w3(e, fq, tb, dh, ld_w3, hid_ready))

        ffn_gen = ffn_units()
        TH["pending"] = None

        def pop_ffn(budget, w3_only=False):
            while budget > 0 and not TH["done"]:
                if TH["pending"] is None:
                    try:
                        TH["pending"] = next(ffn_gen)
                    except StopIteration:
                        TH["done"] = True
                        return
                kind, cost, fn = TH["pending"]
                if w3_only and kind != "w3":
                    return
                fn()
                TH["pending"] = None
                budget -= cost

        def drain_ffn():
            pop_ffn(10 ** 12)

        prev_jh_done = None
        for jh in range(2):
            ld_wv_a = OP("gpsimd", dma_cast(
                scbf[0][:].rearrange("p (c j) -> p c j", j=512),
                wv_in.ap()[0:512, jh * 512:(jh + 1) * 512]
                .rearrange("(c p) j -> p c j", p=P)),
                deps=[prev_jh_done] if jh else [], dma_chan="c_wv")
            ld_wv_b = OP("gpsimd", dma_cast(
                scbf[1][:].rearrange("p (c j) -> p c j", j=512),
                wv_in.ap()[512:1024, jh * 512:(jh + 1) * 512]
                .rearrange("(c p) j -> p c j", p=P)),
                deps=[prev_jh_done] if jh else [], dma_chan="c_wv")
            for tb in range(NTB):
                mm = None
                for dc in range(NDC):
                    dd = [xn_tb_done[tb], ld_wv_a, ld_wv_b]
                    if dc == 0:
                        dd.append(TH["pool"])
                    mm = OP("tensor", lambda t, tb=tb, dc=dc: t.matmul(
                        pool[:],
                        xn_T[dc][:, tb * P:(tb + 1) * P],
                        (scbf[0] if dc < 4 else scbf[1])[:, (dc % 4) * 512:(dc % 4 + 1) * 512],
                        start=(dc == 0), stop=False), deps=dd)
                mmb = OP("tensor", lambda t, jh=jh: t.matmul(
                    pool[:], ones1[:],
                    bias_bf[:, 2 * D + jh * 512:2 * D + (jh + 1) * 512],
                    start=False, stop=True), deps=[ld_bias, ms_ones, mm])
                ev = None
                for hh in range(8):
                    h = jh * 8 + hh
                    ev = OP("vector", lambda v, tb=tb, h=h, hh=hh:
                            v.tensor_copy(
                                vstat[tb][:, h * (DH + 1):h * (DH + 1) + DH],
                                pool[:, hh * DH:(hh + 1) * DH]),
                            deps=[mmb, ms_vstat[tb]])
                TH["pool"] = ev
                pop_ffn(COST_H12)
            prev_jh_done = TH["pool"]
        v_all = TH["pool"]

        pair_wslot_free = [None, None]
        c1_slot = [None, None]
        vs_slot = [None, None]
        prev_pair_done = v_all
        prev_clip = None
        qk_ready = None
        prev_head_ev = None
        pv_by_kc_prev = {}
        last_wch_reader = [gates_all, gates_all]
        for pair in range(8):
            slw = pair % 2
            ld_wq = OP("gpsimd", dma_cast(
                wqb[slw][:].rearrange("p (c j) -> p c j", j=P),
                wq_in.ap()[:, pair * P:(pair + 1) * P]
                .rearrange("(c p) j -> p c j", p=P)),
                deps=[pair_wslot_free[slw]], dma_chan="c_wqk" + str(slw))
            ld_wk = OP("gpsimd", dma_cast(
                wkb[slw][:].rearrange("p (c j) -> p c j", j=P),
                wk_in.ap()[:, pair * P:(pair + 1) * P]
                .rearrange("(c p) j -> p c j", p=P)),
                deps=[pair_wslot_free[slw]], dma_chan="c_wqk" + str(slw))
            ld_wo = OP("gpsimd", dma_cast(
                wob[slw][:], wo_in.ap()[pair * P:(pair + 1) * P, :]),
                deps=[prev_pair_done if pair > 0 else None], dma_chan="c_wo")

            prev_ts = None
            for which, wblk, ldw, brow, dstT in (
                    (0, wqb[slw], ld_wq, 0, qT),
                    (1, wkb[slw], ld_wk, 1, kT)):
                for tb in range(NTB):
                    mm = None
                    for dc in range(NDC):
                        dd = [xn_tb_done[tb], ldw]
                        if dc == 0:
                            dd.append(prev_ts if prev_ts is not None
                                      else prev_pair_done)
                            dd.append(prev_clip)
                        mm = OP("tensor", lambda t, tb=tb, dc=dc, wblk=wblk:
                                t.matmul(
                                    pb_st[:, 0:P],
                                    xn_T[dc][:, tb * P:(tb + 1) * P],
                                    wblk[:, dc * P:(dc + 1) * P],
                                    start=(dc == 0), stop=False), deps=dd)
                    mmb = OP("tensor", lambda t, brow=brow, pair=pair:
                             t.matmul(
                                 pb_st[:, 0:P], ones1[:],
                                 bias_bf[:, brow * D + pair * P:
                                         brow * D + (pair + 1) * P],
                                 start=False, stop=True),
                             deps=[ld_bias, ms_ones, mm])
                    cp = OP("vector", lambda v: v.tensor_copy(
                        wch[0][:, 0:P], pb_st[:, 0:P]),
                        deps=[mmb, prev_ts, last_wch_reader[0]])
                    sq2 = OP("vector", lambda v: v.tensor_mul(
                        wch[0][:, P:2 * P], wch[0][:, 0:P], wch[0][:, 0:P]), deps=[cp])
                    rds = []
                    for hl in range(2):
                        rds.append(OP("vector", lambda v, hl=hl: v.reduce_sum(
                            rq2[:, hl:hl + 1],
                            wch[0][:, P + hl * DH:P + (hl + 1) * DH], axis=AX),
                            deps=[sq2, prev_ts]))
                    l1 = OP("scalar", lambda s: s.activation(
                        rq2[:], rq2[:], AF.Ln, bias=tiny_t[:]),
                        deps=rds + [ms_tiny])
                    l2 = OP("vector", lambda v: v.tensor_scalar(
                        rq2[:], rq2[:], L2EPS2, None, ALU.max), deps=[l1])
                    l3 = OP("scalar", lambda s: s.activation(
                        rq2[:], rq2[:], AF.Exp, scale=-0.5), deps=[l2])
                    tss = []
                    for hl in range(2):
                        tss.append(OP("vector", lambda v, hl=hl:
                                      v.tensor_scalar(
                                          qk_tok[:, hl * DH:(hl + 1) * DH],
                                          wch[0][:, hl * DH:(hl + 1) * DH],
                                          rq2[:, hl:hl + 1], None, ALU.mult),
                                      deps=[l3]))
                    prev_ts = tss[-1]
                    tr = OP("tensor", lambda t: t.transpose(
                        pb_tr[:, P:2 * P], qk_tok[:], id_bf[:]),
                        deps=tss + [ld_ident, qk_ready])
                    qk_ready = OP("vector", lambda v, tb=tb, dstT=dstT:
                                  v.tensor_copy(
                                      dstT[:, tb * P:(tb + 1) * P],
                                      pb_tr[:, P:2 * P]), deps=[tr])
                    pop_ffn(2 * COST_W3, w3_only=True)
            qk_all = qk_ready
            pair_wslot_free[slw] = qk_all

            pair_ev = [None, None]
            prev_score_done = {}
            for hl in range(2):
                h = 2 * pair + hl
                head_start_dep = prev_head_ev
                score_done = {}
                pv_by_kc = {}
                clips_done = {}

                def emit_qk_seg(j, seg, hl=hl):
                    nonlocal prev_clip
                    mm = OP("tensor", lambda t, hl=hl, j=j, seg=seg:
                            t.matmul(
                                pb_st[:],
                                kT[hl * DH:(hl + 1) * DH,
                                   j * P:(j + 1) * P],
                                qT[hl * DH:(hl + 1) * DH,
                                   seg * 512:(seg + 1) * 512],
                                start=True, stop=True),
                            deps=[qk_all, prev_clip])
                    dep_slot = []
                    if seg == 0:
                        if (j - 2) in score_done:
                            dep_slot = [score_done[j - 2]]
                        elif (NKC + j - 2) in prev_score_done:
                            dep_slot = [prev_score_done[NKC + j - 2]]
                    prev_clip = OP("vector", lambda v, j=j, seg=seg:
                                   v.tensor_scalar(
                                       ych[j % 2][:,
                                                  seg * 512:(seg + 1) * 512],
                                       pb_st[:], -C_CLIP, C_CLIP,
                                       ALU.max, ALU.min),
                                   deps=[mm] + dep_slot)
                    clips_done[j] = prev_clip

                for seg in range(4):
                    emit_qk_seg(0, seg)
                    pop_ffn(COST_W3, w3_only=True)
                for kc in range(NKC):
                    slc = kc % 2
                    g1 = OP("gpsimd", lambda g, slc=slc: g.tensor_tensor(
                        wch[slc][:], ych[slc][:], ych[slc][:], ALU.mult),
                        deps=[clips_done[kc], last_wch_reader[slc]]
                        if kc < 2 else [clips_done[kc]])
                    s1 = OP("scalar", lambda s, slc=slc: s.activation(
                        wch[slc][:], wch[slc][:], AF.Ln, bias=1.0,
                        scale=-1.0), deps=[g1])
                    s2 = OP("scalar", lambda s, slc=slc: s.activation(
                        wch[slc][:], wch[slc][:], AF.Sigmoid, scale=-0.5),
                        deps=[s1])
                    v2 = OP("vector", lambda v, slc=slc: v.tensor_mul(
                        ych[slc][:], ych[slc][:], wch[slc][:]), deps=[s2])
                    s3 = OP("scalar", lambda s, slc=slc: s.activation(
                        ych[slc][:], ych[slc][:], AF.Arctan), deps=[v2])
                    pop_ffn(3 * COST_H12)
                    s4 = OP("scalar", lambda s, slc=slc: s.activation(
                        ych[slc][:], ych[slc][:], AF.Ln, bias=lnb_t[:],
                        scale=-2.0), deps=[s3, ms_lnb])
                    dep_pv_slot = ([pv_by_kc[kc - 2]] if (kc - 2) in pv_by_kc
                                   else ([pv_by_kc_prev.get(NKC - 2 + kc)]
                                         if (NKC - 2 + kc) in pv_by_kc_prev
                                         else []))
                    sdep = [s4] + dep_pv_slot + [c1_slot[slc]]
                    if pair == 0 and hl == 0 and kc < 2:
                        sdep.append(v_all)
                    s5 = OP("scalar", lambda s, slc=slc: s.activation(
                        scbf[slc][:], ych[slc][:], AF.Exp, scale=-65.0,
                        accum_out=ncc[slc][:]), deps=sdep)
                    score_done[kc] = s5
                    c1 = OP("scalar", lambda s, slc=slc: s.activation(
                        ccc[slc][:], ncc[slc][:], AF.Ln, bias=tiny_t[:],
                        scale=float(2.0 ** 72)),
                        deps=[s5, ms_tiny, vs_slot[slc]])
                    c1_slot[slc] = c1
                    c2 = OP("scalar", lambda s, slc=slc: s.activation(
                        ccc[slc][:], ccc[slc][:], AF.Exp, scale=-0.5),
                        deps=[c1])
                    vs = OP("vector", lambda v, slc=slc, kc=kc, h=h:
                            v.tensor_scalar(
                                vsc[slc][:],
                                vstat[kc][:,
                                          h * (DH + 1):(h + 1) * (DH + 1)],
                                ccc[slc][:], None, ALU.mult),
                            deps=[c2, v_all] + dep_pv_slot)
                    vs_slot[slc] = vs
                    pv_last = None
                    for u in range(16):
                        dd = [vs, s5]
                        if kc == 0:
                            dd.append(head_start_dep)
                        st_flag = (kc == 0 and u in (0, 7, 14))
                        sp_flag = (kc == NKC - 1 and u in (6, 13, 15))
                        pv_last = OP("tensor", lambda t, slc=slc, u=u, kc=kc,
                                     st_flag=st_flag, sp_flag=sp_flag:
                                     t.matmul(
                                         pvt_unit(u),
                                         scbf[slc][:, u * P:(u + 1) * P],
                                         vsc[slc][:],
                                         start=st_flag, stop=sp_flag,
                                         skip_group_check=True), deps=dd)
                        if u % 4 == 3 and kc + 1 < NKC:
                            emit_qk_seg(kc + 1, u // 4)
                    pv_by_kc[kc] = pv_last
                    last_wch_reader[slc] = v2
                    pop_ffn(4 * COST_W3, w3_only=True)
                pv_by_kc_prev = pv_by_kc
                prev_score_done = score_done
                ev_last = None
                for u in range(16):
                    fr = OP("vector", lambda v, u=u: v.reciprocal(
                        fact[:], pvt_unit(u)[:, DH:DH + 1]),
                        deps=[pv_by_kc[NKC - 1], ev_last])
                    ev_last = OP("vector", lambda v, u=u, hl=hl:
                                 v.tensor_scalar(
                                     attnp[:, u * P + hl * DH:
                                           u * P + (hl + 1) * DH],
                                     pvt_unit(u)[:, 0:DH], fact[:],
                                     None, ALU.mult), deps=[fr])
                pair_ev[hl] = ev_last
                prev_head_ev = ev_last

            prev_cpo = None
            last_ad = None
            for u in range(NTB):
                tro = OP("tensor", lambda t, u=u: t.transpose(
                    pb_tr[:, 2 * P:3 * P], attnp[:, u * P:(u + 1) * P],
                    id_bf[:]),
                    deps=[pair_ev[0], pair_ev[1], ld_ident, prev_cpo])
                prev_cpo = OP("vector", lambda v, u=u: v.tensor_copy(
                    oT[u % 2][:], pb_tr[:, 2 * P:3 * P]), deps=[tro])
                for dh in range(2):
                    mo = OP("tensor", lambda t, u=u, dh=dh, slw=slw,
                            pair=pair: t.matmul(
                                pool[:], oT[u % 2][:],
                                wob[slw][:, dh * 512:(dh + 1) * 512],
                                start=True, stop=(pair != 0)),
                            deps=[prev_cpo, ld_wo, TH["pool"]])
                    if pair == 0:
                        mo = OP("tensor", lambda t, dh=dh: t.matmul(
                            pool[:], ones1[:],
                            bias_bf[:, 3 * D + dh * 512:3 * D + (dh + 1) * 512],
                            start=False, stop=True), deps=[mo, ld_bias])
                    ad = OP("vector", lambda v, u=u, dh=dh:
                            v.scalar_tensor_tensor(
                                acc[u][:, dh * 512:(dh + 1) * 512],
                                pool[:], 1.0,
                                acc[u][:, dh * 512:(dh + 1) * 512],
                                ALU.mult, ALU.add),
                            deps=[mo, ms_acc[u]])
                    TH["pool"] = ad
                    last_ad = ad
                pop_ffn(2 * COST_W3, w3_only=True)
            prev_pair_done = last_ad
        attn_all = prev_pair_done

        drain_ffn()
        ffn_all = TH["ffn_last"]

        if KDBG:
            def dump(nm, ap_src, dep):
                return OP("gpsimd", lambda sy, nm=nm, ap_src=ap_src:
                          sy.dma_start(dbg[nm].ap(), ap_src), deps=[dep],
                          dma_chan="dsto")
            dump("d_xnT0", xn_T[0][:], xn_all)
            dump("d_gates", gates[:], gates_all)
            dump("d_qT", qT[:], attn_all)
            dump("d_kT", kT[:], attn_all)
            dump("d_vst0", vstat[0][:], attn_all)
            dump("d_attnp", attnp[:], attn_all)
            dump("d_hid0", hid[0][:], ffn_all)
            OP("gpsimd", lambda g: g.dma_start(dbg["d_acc0f"].ap(), acc[0][:]),
               deps=[attn_all, ffn_all], dma_chan="dsto")
            attn_all = dump("d_acc0a", acc[0][:], attn_all)

        last_st = None
        for tb in range(NTB):
            sl = 0
            dx = OP("sync", lambda s, tb=tb, sl=sl: s.dma_start(
                xst[sl], x_in.ap()[tb * P:(tb + 1) * P, :]),
                deps=[last_st, ffn_all, attn_all], dma_chan="dxin")
            ao = OP("vector", lambda v, tb=tb, sl=sl: v.tensor_add(
                xst[sl], xst[sl], acc[tb][:]), deps=[dx, ffn_all, attn_all] +
                ([last_st] if last_st else []))
            st = OP("sync", lambda s, tb=tb, sl=sl: s.dma_start(
                out_t.ap()[tb * P:(tb + 1) * P, :], xst[sl]),
                deps=[ao], dma_chan="dsto")
            last_st = st

        sch.emit_all(block, sems)

        @block.sync
        def _(sync):
            sync.wait_ge(sems["dsto"], sch.counts["dsto"])

        pctx.close()

    return nc


_NC_CACHE = {}


def kernel(L0_attn_x, L0_attn_wq, L0_attn_bq, L0_attn_wk, L0_attn_bk,
           L0_attn_wv, L0_attn_bv, L0_attn_wo, L0_attn_bo,
           L0_ffn_Wg, L0_ffn_W1, L0_ffn_W2, L0_ffn_W3):
    x = np.ascontiguousarray(np.asarray(L0_attn_x, dtype=np.float32))
    wq = np.ascontiguousarray(np.asarray(L0_attn_wq, dtype=np.float32))
    wk = np.ascontiguousarray(np.asarray(L0_attn_wk, dtype=np.float32))
    wv = np.ascontiguousarray(np.asarray(L0_attn_wv, dtype=np.float32))
    wo = np.ascontiguousarray(np.asarray(L0_attn_wo, dtype=np.float32))
    bqkv = np.ascontiguousarray(np.stack([
        np.asarray(L0_attn_bq, dtype=np.float32),
        np.asarray(L0_attn_bk, dtype=np.float32),
        np.asarray(L0_attn_bv, dtype=np.float32),
        np.asarray(L0_attn_bo, dtype=np.float32)], axis=0))
    wgt = np.ascontiguousarray(np.asarray(L0_ffn_Wg, dtype=np.float32).T)
    w1t = np.ascontiguousarray(
        np.asarray(L0_ffn_W1, dtype=np.float32).transpose(0, 2, 1))
    w2t = np.ascontiguousarray(
        np.asarray(L0_ffn_W2, dtype=np.float32).transpose(0, 2, 1))
    w3t = np.ascontiguousarray(
        np.asarray(L0_ffn_W3, dtype=np.float32).transpose(0, 2, 1))
    ident = np.eye(P, dtype=np.float32)

    if "nc" not in _NC_CACHE:
        _NC_CACHE["nc"] = build_core()
    nc = _NC_CACHE["nc"]

    in_maps = []
    for c in range(8):
        in_maps.append({
            "x": np.ascontiguousarray(x[c]),
            "wq": wq, "wk": wk, "wv": wv, "wo": wo,
            "bqkv": bqkv, "wgt": wgt,
            "w1t": w1t, "w2t": w2t, "w3t": w3t,
            "ident": ident,
        })
    res = run_bass_kernel_spmd(nc, in_maps, core_ids=list(range(8)))
    out = np.stack([res.results[c]["out"] for c in range(8)], axis=0)
    return out.astype(np.float32)


# version 09
# speedup vs baseline: 1.1099x; 1.0932x over previous
import math
import os
from contextlib import ExitStack

import numpy as np
import ml_dtypes

import concourse.bass as bass
import concourse.mybir as mybir
from concourse.bass_utils import run_bass_kernel_spmd

F32 = mybir.dt.float32
BF16 = mybir.dt.bfloat16
AF = mybir.ActivationFunctionType
ALU = mybir.AluOpType
AX = mybir.AxisListType.X

P = 128
S = 2048
D = 1024
H = 16
DH = 64
E8 = 8
FF = 2048
NTB = S // P
NDC = D // P
NKC = S // P
C_CLIP = float(np.float32(1.0 - 1e-6))
LN_BIAS = float(1.0 + math.pi / 2.0)
L2EPS2 = float(2.0 * math.log(1e-12))


class Sched:

    ENGS = ("sync", "gpsimd", "scalar", "vector", "tensor")

    def __init__(self):
        self.ops = []
        self.counts = {}

    SAME_ENG_WINDOW = 16

    def op(self, eng, emit, deps=(), dma_chan=None):
        chan = dma_chan if dma_chan is not None else eng
        inc = 16 if dma_chan is not None else 1
        n = self.counts.get(chan, 0) + inc
        self.counts[chan] = n
        ordn = self.counts.get("_ord_" + eng, 0) + 1
        self.counts["_ord_" + eng] = ordn
        rec = {"eng": eng, "emit": emit,
               "deps": [d for d in deps if d is not None],
               "chan": chan, "seq": n, "ord": ordn}
        self.ops.append(rec)
        return rec

    def emit_all(self, block, sems):
        per_eng = {e: [] for e in self.ENGS}
        for rec in self.ops:
            per_eng[rec["eng"]].append(rec)
        for eng_name in self.ENGS:
            recs = per_eng[eng_name]
            if not recs:
                continue

            def body(engine, recs=recs, eng_name=eng_name):
                high = {}
                for rec in recs:
                    for dep in rec["deps"]:
                        ch, val = dep["chan"], dep["seq"]
                        if ch == eng_name and dep["eng"] == eng_name:
                            if rec["ord"] - dep["ord"] > self.SAME_ENG_WINDOW:
                                continue
                        if high.get(ch, 0) >= val:
                            continue
                        engine.wait_ge(sems[ch], val)
                        high[ch] = val
                    instr = rec["emit"](engine)
                    instr.then_inc(
                        sems[rec["chan"]],
                        16 if rec["chan"] not in self.ENGS else 1)
            getattr(block, eng_name)(body)


def build_core():
    nc = bass.Bass("TRN2", target_bir_lowering=False,
                   detect_race_conditions=False,
                   dynamic_dma_scratch_size=16384)

    x_in = nc.dram_tensor("x", [S, D], F32, kind="ExternalInput")
    wq_in = nc.dram_tensor("wq", [D, D], F32, kind="ExternalInput")
    wk_in = nc.dram_tensor("wk", [D, D], F32, kind="ExternalInput")
    wv_in = nc.dram_tensor("wv", [D, D], F32, kind="ExternalInput")
    wo_in = nc.dram_tensor("wo", [D, D], F32, kind="ExternalInput")
    b_in = nc.dram_tensor("bqkv", [4, D], F32, kind="ExternalInput")
    wgt_in = nc.dram_tensor("wgt", [D, E8], F32, kind="ExternalInput")
    w1t_in = nc.dram_tensor("w1t", [E8, D, FF], F32, kind="ExternalInput")
    w2t_in = nc.dram_tensor("w2t", [E8, D, FF], F32, kind="ExternalInput")
    w3t_in = nc.dram_tensor("w3t", [E8, FF, D], F32, kind="ExternalInput")
    id_in = nc.dram_tensor("ident", [P, P], F32, kind="ExternalInput")
    out_t = nc.dram_tensor("out", [S, D], F32, kind="ExternalOutput")
    KDBG = bool(os.environ.get("KDBG_DUMP"))
    dbg = {}
    if KDBG:
        for nm, shp in (("d_xnT0", [P, S]), ("d_gates", [P, NTB * E8]),
                        ("d_qT", [P, S]), ("d_kT", [P, S]),
                        ("d_vst0", [P, H * (DH + 1)]), ("d_attnp", [P, S]),
                        ("d_acc0a", [P, D]), ("d_acc0f", [P, D]),
                        ("d_ln0", [P, 32]), ("d_g0", [P, 64]), ("d_ncc", [P, 2]),
                        ("d_scbf0", [P, S]), ("d_hid0", [P, S]),
                        ("d_vstall", [P, NTB * H * (DH + 1)]),
                        ("d_scall", [P, NKC * S]),
                        ("d_vscall", [P, NKC * (DH + 1)]),
                        ("d_nccall", [P, NKC]), ("d_cccall", [P, NKC]),
                        ("d_pvt", [P, 3 * 512]), ("d_factall", [P, 16])):
            dbg[nm] = nc.dram_tensor(nm, shp, F32, kind="ExternalOutput")

    sch = Sched()
    OP = sch.op

    with ExitStack() as ctx:
        def sbt(name, shape, dt=F32):
            return ctx.enter_context(nc.sbuf_tensor(name, shape, dt))

        xn_T = [sbt(f"xnT{d}", [P, S], BF16) for d in range(NDC)]
        vstat = [sbt(f"vst{t}", [P, H * (DH + 1)], BF16) for t in range(NTB)]
        acc = [sbt(f"acc{t}", [P, D], BF16) for t in range(NTB)]
        gates = sbt("gates", [P, NTB * E8], F32)
        qT = sbt("qT", [P, S], BF16)
        kT = sbt("kT", [P, S], BF16)
        qk_tok = sbt("qk_tok", [P, P], BF16)
        attnp = sbt("attnp", [P, S], BF16)
        oT = [sbt(f"oT{i}", [P, P], BF16) for i in range(2)]
        ych = [sbt(f"ych{i}", [P, S], F32) for i in range(2)]
        xst = [ych[1][:, D:2 * D]] * 2
        wch = [sbt(f"wch{i}", [P, S], F32) for i in range(2)]
        scbf = [sbt(f"scbf{i}", [P, S], BF16) for i in range(2)]
        ncc = [sbt(f"ncc{i}", [P, 1], F32) for i in range(2)]
        ccc = [sbt(f"ccc{i}", [P, 1], F32) for i in range(2)]
        vsc = [sbt(f"vsc{i}", [P, DH + 1], BF16) for i in range(2)]
        fact = sbt("fact", [P, 1], F32)
        hid = [sbt(f"hid{f}", [P, S], BF16) for f in range(4)]
        wqb = [sbt(f"wqb{i}", [P, D], BF16) for i in range(2)]
        wkb = [sbt(f"wkb{i}", [P, D], BF16) for i in range(2)]
        wob = [sbt("wob0", [P, D], BF16)] * 2
        w1b = [sbt(f"w1b{i}", [P, D], BF16) for i in range(2)]
        w2b = [sbt(f"w2b{i}", [P, D], BF16) for i in range(2)]
        w3b = [sbt(f"w3b{i}", [P, D], BF16) for i in range(4)]
        bias_bf = sbt("bias_bf", [1, 4 * D], BF16)
        wgt_bf = sbt("wgt_bf", [P, NDC * E8], BF16)
        id_bf = sbt("id_bf", [P, P], BF16)
        ones1 = sbt("ones1", [1, P], BF16)
        zrow = sbt("zrow", [1, 512], BF16)
        xn_tok = sbt("xn_tok", [P, D], BF16)
        lnsc = [sbt(f"lnsc{t}", [P, 8], F32) for t in range(NTB)]
        lnb_t = sbt("lnb_t", [P, 1], F32)
        eps_t = sbt("eps_t", [P, 1], F32)
        tiny_t = sbt("tiny_t", [P, 1], F32)
        rq2 = sbt("rq2", [P, 2], F32)
        gsc = [sbt(f"gsc{t}", [P, 40], F32) for t in range(NTB)]

        sems = {}
        for chan in ("sync", "gpsimd", "scalar", "vector", "tensor",
                     "dxin", "c_ident", "c_bias", "c_wgt", "c_wv",
                     "c_wqk0", "c_wqk1", "c_wo", "c_w12_0", "c_w12_1",
                     "c_w3", "dsto"):
            sems[chan] = ctx.enter_context(nc.semaphore("sem_" + chan))

        block = ctx.enter_context(nc.Block())

        def dma_cast(dst_ap, src_ap):
            return lambda g: g.dma_start(dst_ap, src_ap)

        ld_ident = OP("gpsimd", dma_cast(id_bf[:], id_in.ap()), dma_chan="c_ident")
        ld_bias = OP("gpsimd", dma_cast(
            bias_bf[:], bass.AP(b_in, 0, [[4 * D, 1], [1, 4 * D]])), dma_chan="c_bias")
        ld_wgt = OP("gpsimd", dma_cast(
            wgt_bf[:].rearrange("p (c e) -> p c e", e=E8),
            wgt_in.ap().rearrange("(c p) e -> p c e", p=P)),
            dma_chan="c_wgt")

        ms_ones = OP("vector", lambda v: v.memset(ones1[:], 1.0))
        ms_zrow = OP("vector", lambda v: v.memset(zrow[:], 0.0))
        ms_lnb = OP("vector", lambda v: v.memset(lnb_t[:], LN_BIAS))
        ms_eps = OP("vector", lambda v: v.memset(eps_t[:], 1e-5))
        ms_tiny = OP("vector", lambda v: v.memset(tiny_t[:], 1e-38))
        ms_vstat = [OP("vector", lambda v, t=t: v.memset(vstat[t][:], 1.0))
                    for t in range(NTB)]
        ms_acc = [OP("vector", lambda v, t=t: v.memset(acc[t][:], 0.0))
                  for t in range(NTB)]

        pctx = ExitStack()
        pb_st = pctx.enter_context(nc.psum_tensor("pb_st", [P, 1024], F32))
        pvt = [pctx.enter_context(nc.psum_tensor(f"pvt{i}", [P, 512], F32))
               for i in range(3)]
        pb_a = pctx.enter_context(nc.psum_tensor("pb_a", [P, 512], F32))
        pb_b = pctx.enter_context(nc.psum_tensor("pb_b", [P, 512], F32))
        pb_tr = pctx.enter_context(nc.psum_tensor("pb_tr", [P, 384], BF16))

        def pvt_unit(u):
            t = u // 7
            off = (u - t * 7) * (DH + 2)
            return pvt[t][:, off:off + DH + 1]

        prev_xn = None
        prev_lv = None
        prev_tr = None
        xn_tb_done = []
        last_cp = None
        for tb in range(NTB):
            sl = tb % 2
            dx = OP("sync", lambda s, tb=tb, sl=sl: s.dma_start(
                xst[sl], x_in.ap()[tb * P:(tb + 1) * P, :]),
                deps=[prev_xn] if tb >= 1 else [], dma_chan="dxin")
            L = lnsc[tb]
            r1 = OP("vector", lambda v, sl=sl, L=L: v.reduce_sum(
                L[:, 0:1], xst[sl], axis=AX), deps=[dx, prev_xn])
            sq = OP("vector", lambda v, sl=sl: v.tensor_mul(
                ych[0][:, 0:D], xst[sl], xst[sl]), deps=[dx])
            r2 = OP("vector", lambda v, L=L: v.reduce_sum(
                L[:, 1:2], ych[0][:, 0:D], axis=AX), deps=[sq])
            m1 = OP("vector", lambda v, L=L: v.tensor_scalar(
                L[:, 2:3], L[:, 0:1], 1.0 / D, None, ALU.mult), deps=[r1])
            v1op = OP("vector", lambda v, L=L: v.tensor_scalar(
                L[:, 4:5], L[:, 1:2], 1.0 / D, None, ALU.mult),
                deps=[r2])
            em2 = OP("vector", lambda v, L=L: v.scalar_tensor_tensor(
                L[:, 3:4], L[:, 2:3], -1.0, L[:, 2:3], ALU.mult, ALU.mult),
                deps=[m1])
            vfin = OP("vector", lambda v, L=L: v.tensor_add(
                L[:, 4:5], L[:, 4:5], L[:, 3:4]), deps=[v1op, em2])
            lv = OP("scalar", lambda s, L=L: s.activation(
                L[:, 5:6], L[:, 4:5], AF.Ln, bias=eps_t[:]),
                deps=[vfin, ms_eps])
            rst = OP("scalar", lambda s, L=L: s.activation(
                L[:, 5:6], L[:, 5:6], AF.Exp, scale=-0.5), deps=[lv])
            nmr = OP("vector", lambda v, L=L: v.scalar_tensor_tensor(
                L[:, 6:7], L[:, 2:3], -1.0, L[:, 5:6], ALU.mult, ALU.mult),
                deps=[rst])
            xn0 = OP("vector", lambda v, sl=sl, L=L: v.tensor_scalar(
                ych[0][:, 0:D], xst[sl], L[:, 5:6], None, ALU.mult),
                deps=[nmr, prev_tr, r2])
            xn = OP("vector", lambda v, L=L: v.tensor_scalar(
                xn_tok[:], ych[0][:, 0:D], L[:, 6:7], None, ALU.add),
                deps=[xn0])
            prev_xn = xn
            if False and KDBG and tb == 0:
                dbgt = sbt("dbgt", [P, 32])
                dops = []
                dops.append(OP("vector", lambda v: v.tensor_copy(dbgt[:, 0:1], red1[:]), deps=[r1]))
                dops.append(OP("vector", lambda v: v.tensor_copy(dbgt[:, 1:2], red2[:]), deps=[r2]))
                dops.append(OP("vector", lambda v: v.tensor_copy(dbgt[:, 2:3], mean1[:]), deps=[m1]))
                dops.append(OP("vector", lambda v: v.tensor_copy(dbgt[:, 3:4], var1[:]), deps=[vfin]))
                dops.append(OP("vector", lambda v: v.tensor_copy(dbgt[:, 4:5], rstd1[:]), deps=[rst]))
                dops.append(OP("vector", lambda v: v.tensor_copy(dbgt[:, 5:6], nmr1[:]), deps=[nmr]))
                dops.append(OP("vector", lambda v, sl=sl: v.tensor_copy(dbgt[:, 8:16], xst[sl][:, 0:8]), deps=[dx]))
                dops.append(OP("vector", lambda v: v.tensor_copy(dbgt[:, 16:24], xn_tok[:, 0:8]), deps=[xn]))
                OP("gpsimd", lambda g: g.dma_start(dbg["d_ln0"].ap(), dbgt[:]),
                   deps=dops, dma_chan="dsto")
            for dc in range(NDC):
                tr = OP("tensor", lambda t, dc=dc: t.transpose(
                    pb_tr[:, 0:P], xn_tok[:, dc * P:(dc + 1) * P], id_bf[:]),
                    deps=[xn, ld_ident, last_cp])
                prev_tr = tr
                last_cp = OP("vector", lambda v, dc=dc, tb=tb: v.tensor_copy(
                    xn_T[dc][:, tb * P:(tb + 1) * P], pb_tr[:, 0:P]),
                    deps=[tr])
            xn_tb_done.append(last_cp)
        xn_all = last_cp

        gmm_last = None
        for tb in range(NTB):
            for dc in range(NDC):
                gmm_last = OP("tensor", lambda t, tb=tb, dc=dc: t.matmul(
                    pb_a[:, tb * E8:(tb + 1) * E8],
                    xn_T[dc][:, tb * P:(tb + 1) * P],
                    wgt_bf[:, dc * E8:(dc + 1) * E8],
                    start=(dc == 0), stop=(dc == NDC - 1)),
                    deps=[xn_tb_done[tb], ld_wgt])
        gev = OP("vector", lambda v: v.tensor_copy(wch[1][:, 0:P], pb_a[:, 0:P]),
                 deps=[gmm_last])
        gate_done = []
        for tb in range(NTB):
            gsl = wch[1][:, tb * E8:(tb + 1) * E8]
            G = gsc[tb]
            gv1 = G[:, 0:1]; gnv1 = G[:, 1:2]; gv2 = G[:, 2:3]
            gden = G[:, 3:4]; grd = G[:, 4:5]
            gs1 = G[:, 8:16]; gt1 = G[:, 16:24]; gt2 = G[:, 24:32]
            ge1 = G[:, 32:40]
            a1 = OP("vector", lambda v, gsl=gsl, gv1=gv1: v.reduce_max(
                gv1, gsl, axis=AX), deps=[gev])
            a2 = OP("scalar", lambda s, gsl=gsl, gs1=gs1, gv1=gv1: s.activation(
                gs1, gsl, AF.Relu, bias=gv1, scale=-1.0), deps=[a1])
            a3 = OP("scalar", lambda s, gs1=gs1: s.activation(
                gs1, gs1, AF.Sign), deps=[a2])
            a4 = OP("vector", lambda v, gsl=gsl, gt1=gt1, gs1=gs1: v.tensor_mul(
                gt1, gsl, gs1), deps=[a3])
            a5 = OP("vector", lambda v, gt2=gt2, gs1=gs1: v.tensor_scalar(
                gt2, gs1, 1e30, -1e30, ALU.mult, ALU.add), deps=[a3])
            a6 = OP("vector", lambda v, gt1=gt1, gt2=gt2: v.tensor_add(
                gt1, gt1, gt2), deps=[a4, a5])
            a7 = OP("vector", lambda v, gv2=gv2, gt1=gt1: v.reduce_max(
                gv2, gt1, axis=AX), deps=[a6])
            a8 = OP("vector", lambda v, gnv1=gnv1, gv1=gv1: v.tensor_scalar(
                gnv1, gv1, -1.0, None, ALU.mult), deps=[a1])
            a9 = OP("scalar", lambda s, gsl=gsl, ge1=ge1, gnv1=gnv1: s.activation(
                ge1, gsl, AF.Exp, bias=gnv1), deps=[a8])
            a10 = OP("scalar", lambda s, gsl=gsl, gt2=gt2, gv2=gv2: s.activation(
                gt2, gsl, AF.Relu, bias=gv2, scale=-1.0), deps=[a7])
            a11 = OP("scalar", lambda s, gt2=gt2: s.activation(
                gt2, gt2, AF.Sign), deps=[a10])
            a12 = OP("vector", lambda v, gt2=gt2, ge1=ge1: v.tensor_mul(
                gt2, ge1, gt2), deps=[a11, a9])
            a13 = OP("vector", lambda v, ge1=ge1, gt2=gt2: v.tensor_sub(
                ge1, ge1, gt2), deps=[a12])
            a14 = OP("scalar", lambda s, gden=gden, gv2=gv2, gnv1=gnv1: s.activation(
                gden, gv2, AF.Exp, bias=gnv1), deps=[a7])
            a15 = OP("vector", lambda v, gden=gden: v.tensor_scalar(
                gden, gden, 1.0, None, ALU.add), deps=[a14])
            a16 = OP("vector", lambda v, grd=grd, gden=gden: v.reciprocal(
                grd, gden), deps=[a15])
            a17 = OP("scalar", lambda s, tb=tb, ge1=ge1, grd=grd: s.activation(
                gates[:, tb * E8:(tb + 1) * E8], ge1, AF.Copy,
                scale=grd), deps=[a16, a13])
            gate_done.append(a17)
        gates_all = gate_done[-1]

        v_done = [gev]
        for jh in range(2):
            ld_wv_a = OP("gpsimd", dma_cast(
                scbf[0][:].rearrange("p (c j) -> p c j", j=512),
                wv_in.ap()[0:512, jh * 512:(jh + 1) * 512]
                .rearrange("(c p) j -> p c j", p=P)),
                deps=[v_done[-1]] if jh else [], dma_chan="c_wv")
            ld_wv_b = OP("gpsimd", dma_cast(
                scbf[1][:].rearrange("p (c j) -> p c j", j=512),
                wv_in.ap()[512:1024, jh * 512:(jh + 1) * 512]
                .rearrange("(c p) j -> p c j", p=P)),
                deps=[v_done[-1]] if jh else [], dma_chan="c_wv")
            ld_wv = ld_wv_b
            for tb in range(NTB):
                pb = pb_a if tb % 2 == 0 else pb_b
                mm = None
                for dc in range(NDC):
                    mm = OP("tensor", lambda t, tb=tb, dc=dc, pb=pb: t.matmul(
                        pb[:],
                        xn_T[dc][:, tb * P:(tb + 1) * P],
                        (scbf[0] if dc < 4 else scbf[1])[:, (dc % 4) * 512:(dc % 4 + 1) * 512],
                        start=(dc == 0), stop=False),
                        deps=[xn_tb_done[tb], ld_wv_a, ld_wv_b] +
                        ([v_done[-2]] if dc == 0 and len(v_done) >= 2 else
                         ([v_done[-1]] if dc == 0 else [])))
                mmb = OP("tensor", lambda t, jh=jh, pb=pb: t.matmul(
                    pb[:], ones1[:], bias_bf[:, 2 * D + jh * 512:2 * D + (jh + 1) * 512],
                    start=False, stop=True), deps=[ld_bias, ms_ones, mm])
                for hh in range(8):
                    h = jh * 8 + hh
                    ev = OP("vector", lambda v, tb=tb, h=h, hh=hh, pb=pb:
                            v.tensor_copy(
                                vstat[tb][:, h * (DH + 1):h * (DH + 1) + DH],
                                pb[:, hh * DH:(hh + 1) * DH]),
                            deps=[mmb, ms_vstat[tb]])
                    v_done.append(ev)
        v_all = v_done[-1]

        pair_wslot_free = [None, None]
        pair_woslot_free = [None, None]
        c1_slot = [None, None]
        vs_slot = [None, None]
        prev_pair_done = v_all
        prev_clip = None
        qk_ready = None
        prev_head_ev = None
        pv_by_kc_prev = {}
        for pair in range(8):
            slw = pair % 2
            ld_wq = OP("gpsimd", dma_cast(
                wqb[slw][:].rearrange("p (c j) -> p c j", j=P),
                wq_in.ap()[:, pair * P:(pair + 1) * P]
                .rearrange("(c p) j -> p c j", p=P)),
                deps=[pair_wslot_free[slw]], dma_chan="c_wqk" + str(slw))
            ld_wk = OP("gpsimd", dma_cast(
                wkb[slw][:].rearrange("p (c j) -> p c j", j=P),
                wk_in.ap()[:, pair * P:(pair + 1) * P]
                .rearrange("(c p) j -> p c j", p=P)),
                deps=[pair_wslot_free[slw]], dma_chan="c_wqk" + str(slw))
            ld_wo = OP("gpsimd", dma_cast(
                wob[slw][:], wo_in.ap()[pair * P:(pair + 1) * P, :]),
                deps=[prev_pair_done if pair > 0 else None], dma_chan="c_wo")

            prev_ts = None
            for which, wblk, ldw, brow, dstT in (
                    (0, wqb[slw], ld_wq, 0, qT),
                    (1, wkb[slw], ld_wk, 1, kT)):
                for tb in range(NTB):
                    mm = None
                    for dc in range(NDC):
                        dd = [xn_tb_done[tb], ldw]
                        if dc == 0:
                            dd.append(prev_ts if prev_ts is not None
                                      else prev_pair_done)
                        mm = OP("tensor", lambda t, tb=tb, dc=dc, wblk=wblk:
                                t.matmul(
                                    pb_a[:, 0:P],
                                    xn_T[dc][:, tb * P:(tb + 1) * P],
                                    wblk[:, dc * P:(dc + 1) * P],
                                    start=(dc == 0), stop=False), deps=dd)
                    mmb = OP("tensor", lambda t, brow=brow, pair=pair:
                             t.matmul(
                                 pb_a[:, 0:P], ones1[:],
                                 bias_bf[:, brow * D + pair * P:
                                         brow * D + (pair + 1) * P],
                                 start=False, stop=True),
                             deps=[ld_bias, ms_ones, mm])
                    cp = OP("vector", lambda v: v.tensor_copy(
                        wch[0][:, 0:P], pb_a[:, 0:P]), deps=[mmb, prev_ts])
                    sq2 = OP("vector", lambda v: v.tensor_mul(
                        wch[0][:, P:2 * P], wch[0][:, 0:P], wch[0][:, 0:P]), deps=[cp])
                    rds = []
                    for hl in range(2):
                        rds.append(OP("vector", lambda v, hl=hl: v.reduce_sum(
                            rq2[:, hl:hl + 1],
                            wch[0][:, P + hl * DH:P + (hl + 1) * DH], axis=AX),
                            deps=[sq2, prev_ts]))
                    l1 = OP("scalar", lambda s: s.activation(
                        rq2[:], rq2[:], AF.Ln, bias=tiny_t[:]),
                        deps=rds + [ms_tiny])
                    l2 = OP("vector", lambda v: v.tensor_scalar(
                        rq2[:], rq2[:], L2EPS2, None, ALU.max), deps=[l1])
                    l3 = OP("scalar", lambda s: s.activation(
                        rq2[:], rq2[:], AF.Exp, scale=-0.5), deps=[l2])
                    tss = []
                    for hl in range(2):
                        tss.append(OP("vector", lambda v, hl=hl:
                                      v.tensor_scalar(
                                          qk_tok[:, hl * DH:(hl + 1) * DH],
                                          wch[0][:, hl * DH:(hl + 1) * DH],
                                          rq2[:, hl:hl + 1], None, ALU.mult),
                                      deps=[l3]))
                    prev_ts = tss[-1]
                    tr = OP("tensor", lambda t: t.transpose(
                        pb_tr[:, P:2 * P], qk_tok[:], id_bf[:]),
                        deps=tss + [ld_ident, qk_ready])
                    qk_ready = OP("vector", lambda v, tb=tb, dstT=dstT:
                                  v.tensor_copy(
                                      dstT[:, tb * P:(tb + 1) * P],
                                      pb_tr[:, P:2 * P]), deps=[tr])
            qk_all = qk_ready
            pair_wslot_free[slw] = qk_all

            pair_ev = [None, None]
            for hl in range(2):
                h = 2 * pair + hl
                head_start_dep = prev_head_ev
                score_done = {}
                pv_by_kc = {}
                dbg_guard = {}
                for kc in range(NKC):
                    slc = kc % 2
                    qk_mms = []
                    for qh in range(2):
                        for half in range(2):
                            qk_mms.append(OP("tensor", lambda t, hl=hl,
                                             kc=kc, qh=qh, half=half:
                                             t.matmul(
                                pb_st[:, half * 512:(half + 1) * 512],
                                kT[hl * DH:(hl + 1) * DH,
                                   kc * P:(kc + 1) * P],
                                qT[hl * DH:(hl + 1) * DH,
                                   qh * 1024 + half * 512:
                                   qh * 1024 + (half + 1) * 512],
                                start=True, stop=True),
                                deps=[qk_all, prev_clip]))
                        dep_slot = ([score_done[kc - 2]]
                                    if (kc - 2) in score_done else [])
                        prev_clip = OP("vector", lambda v, slc=slc, qh=qh:
                                       v.tensor_scalar(
                                           ych[slc][:,
                                                    qh * 1024:(qh + 1) * 1024],
                                           pb_st[:], -C_CLIP, C_CLIP,
                                           ALU.max, ALU.min),
                                       deps=qk_mms[-2:] + dep_slot)
                    g1 = OP("gpsimd", lambda g, slc=slc: g.tensor_tensor(
                        wch[slc][:], ych[slc][:], ych[slc][:], ALU.mult),
                        deps=[prev_clip])
                    s1 = OP("scalar", lambda s, slc=slc: s.activation(
                        wch[slc][:], wch[slc][:], AF.Ln, bias=1.0,
                        scale=-1.0), deps=[g1])
                    s2 = OP("scalar", lambda s, slc=slc: s.activation(
                        wch[slc][:], wch[slc][:], AF.Sigmoid, scale=-0.5),
                        deps=[s1])
                    v2 = OP("vector", lambda v, slc=slc: v.tensor_mul(
                        ych[slc][:], ych[slc][:], wch[slc][:]), deps=[s2])
                    s3 = OP("scalar", lambda s, slc=slc: s.activation(
                        ych[slc][:], ych[slc][:], AF.Arctan), deps=[v2])
                    s4 = OP("scalar", lambda s, slc=slc: s.activation(
                        ych[slc][:], ych[slc][:], AF.Ln, bias=lnb_t[:],
                        scale=-2.0), deps=[s3, ms_lnb])
                    dep_pv_slot = ([pv_by_kc[kc - 2]] if (kc - 2) in pv_by_kc
                                   else ([pv_by_kc_prev.get(NKC - 2 + kc)]
                                         if (NKC - 2 + kc) in pv_by_kc_prev
                                         else []))
                    dbg_g = dbg_guard.get(kc - 2, [])
                    s5 = OP("scalar", lambda s, slc=slc: s.activation(
                        scbf[slc][:], ych[slc][:], AF.Exp, scale=-65.0,
                        accum_out=ncc[slc][:]),
                        deps=[s4] + dep_pv_slot + [c1_slot[slc]] + dbg_g)
                    score_done[kc] = s5
                    c1 = OP("scalar", lambda s, slc=slc: s.activation(
                        ccc[slc][:], ncc[slc][:], AF.Ln, bias=tiny_t[:],
                        scale=float(2.0 ** 72)),
                        deps=[s5, ms_tiny, vs_slot[slc]] + dbg_g)
                    c1_slot[slc] = c1
                    c2 = OP("scalar", lambda s, slc=slc: s.activation(
                        ccc[slc][:], ccc[slc][:], AF.Exp, scale=-0.5),
                        deps=[c1])
                    vs = OP("vector", lambda v, slc=slc, kc=kc, h=h:
                            v.tensor_scalar(
                                vsc[slc][:],
                                vstat[kc][:,
                                          h * (DH + 1):(h + 1) * (DH + 1)],
                                ccc[slc][:], None, ALU.mult),
                            deps=[c2, v_all] + dep_pv_slot + dbg_g)
                    vs_slot[slc] = vs
                    if KDBG and h == 15:
                        d1 = OP("gpsimd", lambda g, slc=slc, kc=kc: g.dma_start(
                            dbg["d_scall"].ap()[:, kc * S:(kc + 1) * S],
                            scbf[slc][:]), deps=[s5], dma_chan="dsto")
                        d2 = OP("gpsimd", lambda g, slc=slc, kc=kc: g.dma_start(
                            dbg["d_vscall"].ap()[:, kc * (DH + 1):
                                                 (kc + 1) * (DH + 1)],
                            vsc[slc][:]), deps=[vs], dma_chan="dsto")
                        d3 = OP("gpsimd", lambda g, slc=slc, kc=kc: g.dma_start(
                            dbg["d_nccall"].ap()[:, kc:kc + 1],
                            ncc[slc][:],
                            allow_slow_non_contiguous=True),
                            deps=[s5], dma_chan="dsto")
                        d4 = OP("gpsimd", lambda g, slc=slc, kc=kc: g.dma_start(
                            dbg["d_cccall"].ap()[:, kc:kc + 1],
                            ccc[slc][:],
                            allow_slow_non_contiguous=True),
                            deps=[c2], dma_chan="dsto")
                        dbg_guard[kc] = [d1, d2, d3, d4]
                    pv_last = None
                    for u in range(16):
                        dd = [vs, s5]
                        if kc == 0:
                            dd.append(head_start_dep)
                        st_flag = (kc == 0 and u in (0, 7, 14))
                        sp_flag = (kc == NKC - 1 and u in (6, 13, 15))
                        pv_last = OP("tensor", lambda t, slc=slc, u=u, kc=kc,
                                     st_flag=st_flag, sp_flag=sp_flag:
                                     t.matmul(
                                         pvt_unit(u),
                                         scbf[slc][:, u * P:(u + 1) * P],
                                         vsc[slc][:],
                                         start=st_flag, stop=sp_flag,
                                         skip_group_check=True), deps=dd)
                    pv_by_kc[kc] = pv_last
                pv_by_kc_prev = pv_by_kc
                if KDBG and h == 15:
                    pvc = [OP("vector", lambda v, i=i: v.tensor_copy(
                        ych[0][:, i * 512:(i + 1) * 512], pvt[i][:]),
                        deps=[pv_by_kc[NKC - 1]]) for i in range(3)]
                    OP("gpsimd", lambda g: g.dma_start(
                        dbg["d_pvt"].ap(), ych[0][:, 0:3 * 512]),
                        deps=pvc, dma_chan="dsto")
                ev_last = None
                for u in range(16):
                    fr = OP("vector", lambda v, u=u: v.reciprocal(
                        fact[:], pvt_unit(u)[:, DH:DH + 1]),
                        deps=[pv_by_kc[NKC - 1], ev_last])
                    if KDBG and h == 15:
                        fc_ = OP("vector", lambda v, u=u: v.tensor_copy(
                            wch[0][:, u:u + 1], fact[:]), deps=[fr])
                    else:
                        fc_ = None
                    ev_last = OP("vector", lambda v, u=u, hl=hl:
                                 v.tensor_scalar(
                                     attnp[:, u * P + hl * DH:
                                           u * P + (hl + 1) * DH],
                                     pvt_unit(u)[:, 0:DH], fact[:],
                                     None, ALU.mult), deps=[fr, fc_])
                if KDBG and h == 15:
                    OP("gpsimd", lambda g: g.dma_start(
                        dbg["d_factall"].ap(), wch[0][:, 0:16]),
                        deps=[ev_last], dma_chan="dsto")
                pair_ev[hl] = ev_last
                prev_head_ev = ev_last

            op_adds = []
            prev_cpo = None
            for u in range(NTB):
                tro = OP("tensor", lambda t, u=u: t.transpose(
                    pb_tr[:, 2 * P:3 * P], attnp[:, u * P:(u + 1) * P],
                    id_bf[:]),
                    deps=[pair_ev[0], pair_ev[1], ld_ident, prev_cpo])
                prev_cpo = OP("vector", lambda v, u=u: v.tensor_copy(
                    oT[u % 2][:], pb_tr[:, 2 * P:3 * P]), deps=[tro])
                for dh in range(2):
                    pb = pb_a if dh == 0 else pb_b
                    dd = [prev_cpo, ld_wo]
                    if len(op_adds) >= 2:
                        dd.append(op_adds[-2])
                    elif pair == 0 and u == 0:
                        dd.append(v_all)
                    mo = OP("tensor", lambda t, u=u, dh=dh, pb=pb, slw=slw,
                            pair=pair: t.matmul(
                                pb[:], oT[u % 2][:],
                                wob[slw][:, dh * 512:(dh + 1) * 512],
                                start=True, stop=(pair != 0)), deps=dd)
                    if pair == 0:
                        mo = OP("tensor", lambda t, dh=dh, pb=pb: t.matmul(
                            pb[:], ones1[:],
                            bias_bf[:, 3 * D + dh * 512:3 * D + (dh + 1) * 512],
                            start=False, stop=True), deps=[mo, ld_bias])
                    ad = OP("vector", lambda v, u=u, dh=dh, pb=pb:
                            v.scalar_tensor_tensor(
                                acc[u][:, dh * 512:(dh + 1) * 512],
                                pb[:], 0.0 if os.environ.get("KDBG_NO_OPROJ") else 1.0,
                                acc[u][:, dh * 512:(dh + 1) * 512],
                                ALU.mult, ALU.add),
                            deps=[mo, ms_acc[u]])
                    op_adds.append(ad)
            pair_woslot_free[slw] = op_adds[-1]
            prev_pair_done = op_adds[-1]
        attn_all = prev_pair_done

        if KDBG:
            def dump(nm, ap_src, dep):
                return OP("gpsimd", lambda sy, nm=nm, ap_src=ap_src:
                          sy.dma_start(dbg[nm].ap(), ap_src), deps=[dep],
                          dma_chan="dsto")
            dump("d_xnT0", xn_T[0][:], xn_all)
            dump("d_gates", gates[:], gates_all)
            dump("d_qT", qT[:], attn_all)
            dump("d_kT", kT[:], attn_all)
            dump("d_vst0", vstat[0][:], attn_all)
            for t in range(NTB):
                OP("gpsimd", lambda g, t=t: g.dma_start(
                    dbg["d_vstall"].ap()[:, t * H * (DH + 1):
                                         (t + 1) * H * (DH + 1)],
                    vstat[t][:]), deps=[attn_all], dma_chan="dsto")
            dump("d_attnp", attnp[:], attn_all)
            dump("d_scbf0", scbf[0][:], attn_all)
            dops2 = [OP("vector", lambda v: v.tensor_copy(rq2[:, 0:1], ncc[0][:]), deps=[attn_all]),
                     OP("vector", lambda v: v.tensor_copy(rq2[:, 1:2], ncc[1][:]), deps=[attn_all])]
            OP("gpsimd", lambda g: g.dma_start(dbg["d_ncc"].ap(), rq2[:]), deps=dops2, dma_chan="dsto")
            attn_all = dump("d_acc0a", acc[0][:], attn_all)

        pctx.close()
        pctx2 = ExitStack()
        fb = [pctx2.enter_context(nc.psum_tensor(f"fb{i}", [P, 512], F32))
              for i in range(4)]

        KDBG_NO_FFN = bool(os.environ.get("KDBG_NO_FFN"))
        last_w12_free = [None, None]
        last_w3_free = [None] * 4
        last_hid_free = [None] * 4
        prev_silu = None
        prev_hidmul = None
        w3_adds = []
        ffn_tb_last = [None] * NTB
        first_ffn = True
        for e in range([0, E8][0 if KDBG_NO_FFN else 1]):
            for fq in range(4):
                ld_w3 = []
                for i in range(4):
                    fc = fq * 4 + i
                    ld_w3.append(OP("gpsimd", dma_cast(
                        w3b[i][:], w3t_in.ap()[e, fc * P:(fc + 1) * P, :]),
                        deps=[last_w3_free[i]], dma_chan="c_w3"))
                hid_ready = [None] * 16
                for i in range(4):
                    fc = fq * 4 + i
                    slf = fc % 2
                    l1d = OP("gpsimd", dma_cast(
                        w1b[slf][:].rearrange("p (c f) -> p c f", f=P),
                        w1t_in.ap()[e, :, fc * P:(fc + 1) * P]
                        .rearrange("(c p) f -> p c f", p=P)),
                        deps=[last_w12_free[slf]], dma_chan="c_w12_" + str(slf))
                    l2d = OP("gpsimd", dma_cast(
                        w2b[slf][:].rearrange("p (c f) -> p c f", f=P),
                        w2t_in.ap()[e, :, fc * P:(fc + 1) * P]
                        .rearrange("(c p) f -> p c f", p=P)),
                        deps=[last_w12_free[slf]], dma_chan="c_w12_" + str(slf))
                    for tq in range(4):
                        mm1 = None
                        for dc in range(NDC):
                            dd = [l1d, xn_all]
                            if dc == 0:
                                dd.append(prev_silu)
                                if first_ffn:
                                    dd.append(attn_all)
                                    first_ffn = False
                            mm1 = OP("tensor", lambda t, slf=slf, dc=dc,
                                     tq=tq: t.matmul(
                                         fb[0][:],
                                         w1b[slf][:, dc * P:(dc + 1) * P],
                                         xn_T[dc][:, tq * 512:(tq + 1) * 512],
                                         start=(dc == 0),
                                         stop=(dc == NDC - 1)), deps=dd)
                        mm2 = None
                        for dc in range(NDC):
                            dd = [l2d, xn_all]
                            if dc == 0:
                                dd.append(prev_hidmul)
                            mm2 = OP("tensor", lambda t, slf=slf, dc=dc,
                                     tq=tq: t.matmul(
                                         fb[1][:],
                                         w2b[slf][:, dc * P:(dc + 1) * P],
                                         xn_T[dc][:, tq * 512:(tq + 1) * 512],
                                         start=(dc == 0),
                                         stop=(dc == NDC - 1)), deps=dd)
                        new_silu = OP("scalar", lambda s: s.activation(
                            ych[1][:, 0:512], fb[0][:], AF.Silu),
                            deps=[mm1, prev_hidmul])
                        prev_silu = new_silu
                        dd = [new_silu, mm2, last_hid_free[i]]
                        prev_hidmul = OP("vector", lambda v, i=i, tq=tq:
                                         v.tensor_mul(
                                             hid[i][:,
                                                    tq * 512:(tq + 1) * 512],
                                             ych[1][:, 0:512], fb[1][:]), deps=dd)
                        hid_ready[i * 4 + tq] = prev_hidmul
                    last_w12_free[slf] = prev_hidmul
                for tb in range(NTB):
                    for dh in range(2):
                        pbw = fb[2 + (tb * 2 + dh) % 2]
                        mm3 = None
                        for i in range(4):
                            dd = [ld_w3[i], hid_ready[i * 4 + tb // 4]]
                            if i == 0 and len(w3_adds) >= 2:
                                dd.append(w3_adds[-2])
                            mm3 = OP("tensor", lambda t, i=i, tb=tb, dh=dh,
                                     pbw=pbw: t.matmul(
                                         pbw[:],
                                         hid[i][:, tb * P:(tb + 1) * P],
                                         w3b[i][:, dh * 512:(dh + 1) * 512],
                                         start=(i == 0), stop=(i == 3)),
                                     deps=dd)
                        ga = OP("vector", lambda v, e=e, tb=tb, dh=dh,
                                pbw=pbw: v.scalar_tensor_tensor(
                                    acc[tb][:, dh * 512:(dh + 1) * 512],
                                    pbw[:],
                                    gates[:, tb * E8 + e:tb * E8 + e + 1],
                                    acc[tb][:, dh * 512:(dh + 1) * 512],
                                    ALU.mult, ALU.add),
                                deps=[mm3, gates_all, attn_all])
                        w3_adds.append(ga)
                        ffn_tb_last[tb] = ga
                for i in range(4):
                    last_w3_free[i] = w3_adds[-1]
                    last_hid_free[i] = w3_adds[-1]
        ffn_all = w3_adds[-1] if w3_adds else attn_all

        if KDBG:
            dump("d_hid0", hid[0][:], ffn_all)
            ffn_all = dump("d_acc0f", acc[0][:], ffn_all)

        last_st = None
        for tb in range(NTB):
            sl = 0
            dx = OP("sync", lambda s, tb=tb, sl=sl: s.dma_start(
                xst[sl], x_in.ap()[tb * P:(tb + 1) * P, :]),
                deps=[last_st, ffn_tb_last[tb]], dma_chan="dxin")
            ao = OP("vector", lambda v, tb=tb, sl=sl: v.tensor_add(
                xst[sl], xst[sl], acc[tb][:]), deps=[dx, ffn_tb_last[tb]] +
                ([last_st] if last_st else []))
            st = OP("sync", lambda s, tb=tb, sl=sl: s.dma_start(
                out_t.ap()[tb * P:(tb + 1) * P, :], xst[sl]),
                deps=[ao], dma_chan="dsto")
            last_st = st

        sch.emit_all(block, sems)

        @block.sync
        def _(sync):
            sync.wait_ge(sems["dsto"], sch.counts["dsto"])

        pctx2.close()

    return nc


_NC_CACHE = {}


def kernel(L0_attn_x, L0_attn_wq, L0_attn_bq, L0_attn_wk, L0_attn_bk,
           L0_attn_wv, L0_attn_bv, L0_attn_wo, L0_attn_bo,
           L0_ffn_Wg, L0_ffn_W1, L0_ffn_W2, L0_ffn_W3):
    x = np.ascontiguousarray(np.asarray(L0_attn_x, dtype=np.float32))
    wq = np.ascontiguousarray(np.asarray(L0_attn_wq, dtype=np.float32))
    wk = np.ascontiguousarray(np.asarray(L0_attn_wk, dtype=np.float32))
    wv = np.ascontiguousarray(np.asarray(L0_attn_wv, dtype=np.float32))
    wo = np.ascontiguousarray(np.asarray(L0_attn_wo, dtype=np.float32))
    bqkv = np.ascontiguousarray(np.stack([
        np.asarray(L0_attn_bq, dtype=np.float32),
        np.asarray(L0_attn_bk, dtype=np.float32),
        np.asarray(L0_attn_bv, dtype=np.float32),
        np.asarray(L0_attn_bo, dtype=np.float32)], axis=0))
    wgt = np.ascontiguousarray(np.asarray(L0_ffn_Wg, dtype=np.float32).T)
    w1t = np.ascontiguousarray(
        np.asarray(L0_ffn_W1, dtype=np.float32).transpose(0, 2, 1))
    w2t = np.ascontiguousarray(
        np.asarray(L0_ffn_W2, dtype=np.float32).transpose(0, 2, 1))
    w3t = np.ascontiguousarray(
        np.asarray(L0_ffn_W3, dtype=np.float32).transpose(0, 2, 1))
    ident = np.eye(P, dtype=np.float32)

    if "nc" not in _NC_CACHE:
        _NC_CACHE["nc"] = build_core()
    nc = _NC_CACHE["nc"]

    in_maps = []
    for c in range(8):
        in_maps.append({
            "x": np.ascontiguousarray(x[c]),
            "wq": wq, "wk": wk, "wv": wv, "wo": wo,
            "bqkv": bqkv, "wgt": wgt,
            "w1t": w1t, "w2t": w2t, "w3t": w3t,
            "ident": ident,
        })
    res = run_bass_kernel_spmd(nc, in_maps, core_ids=list(range(8)))
    out = np.stack([res.results[c]["out"] for c in range(8)], axis=0)
    return out.astype(np.float32)
